# version 04
# baseline (speedup 1.0000x reference)
import numpy as np
from contextlib import ExitStack

import concourse.bass as bass
import concourse.mybir as mybir
from concourse.bass_utils import run_bass_kernel_spmd

F32 = mybir.dt.float32
BF16 = mybir.dt.bfloat16
I32 = mybir.dt.int32
AF = mybir.ActivationFunctionType
ALU = mybir.AluOpType

SEQ = 2
S = 2048
HID = 2048
NH, NKV, DH = 16, 2, 128
NBANDS = 20
MLP = 16384
NCH = 128
SCALE = 0.08838834764831845
CAP = 30.0
EPS = 1e-5
PI = float(np.pi)
PI_LO = float(np.nextafter(np.float32(np.pi), np.float32(0.0)))
INV2PI = float(1.0 / (2.0 * np.pi))
TWO_PI = float(2.0 * np.pi)
LN1E4 = float(np.log(10000.0))

TB = 1024
NSEC = 8
SECC = 16

DEV = ()


def _body(block, eng, fn):
    getattr(block, eng)(fn)


class Ctx:
    def __init__(self, nc, block, sem_names):
        self.nc = nc
        self.block = block
        self.sems = {}
        self.c = {}
        self._es = ExitStack()
        for n in sem_names:
            self.sems[n] = self._es.enter_context(nc.semaphore(n))
            self.c[n] = 0
        self._last = {}

    def W(self, e, ename, sem, val=None):
        if val is None:
            val = self.c[sem]
        if val <= 0:
            return
        key = (ename, sem)
        if self._last.get(key, 0) >= val:
            return
        e.wait_ge(self.sems[sem], val)
        self._last[key] = val

    def I(self, ins, sem, n=1):
        self.c[sem] += n
        ins.then_inc(self.sems[sem], n)
        return self.c[sem]


class Dequant:

    def __init__(self, ctx, bufs, srcs, dsts):
        self.x = ctx
        (self.pk, self.scb, self.lo, self.hi, self.b8, self.mag,
         self.t1, self.t2, self.pl) = bufs
        self.gpk, self.upk, self.dpk, self.gsc, self.usc, self.dsc = srcs
        self.wg_s, self.wu_s, self.wd_s = dsts
        self.start_gate = 0
        self.tiles = []
        for ncg in range(16):
            for w in ("g", "u"):
                for k in range(8):
                    self.tiles.append((w, k, ncg))
            for j in range(4 * ncg, 4 * ncg + 4):
                for hh in range(2):
                    self.tiles.append(("d", j, hh))
        self.n = len(self.tiles)
        self.t0 = 0
        self.li = 0
        self.di = 0
        self.si = 0
        self.load_done = [0] * self.n
        self.dec_done = [0] * self.n
        self.store_done = [0] * self.n
        self.chunk_ready = [0] * NCH

    def _tile_srcs(self, t):
        w, k, ncg = t
        if w in ("g", "u"):
            pk = self.gpk if w == "g" else self.upk
            sc = self.gsc if w == "g" else self.usc
            pk_ap = pk[128 * k:128 * k + 128, 1024 * ncg:1024 * ncg + 1024]
            sc_ap = sc[8 * k:8 * k + 8, 1024 * ncg:1024 * ncg + 1024]
        else:
            j, hh = k, ncg
            pk_ap = self.dpk[128 * j:128 * j + 128, 1024 * hh:1024 * hh + 1024]
            sc_ap = self.dsc[8 * j:8 * j + 8, 1024 * hh:1024 * hh + 1024]
        scb_ap = sc_ap.unsqueeze(1).broadcast_to([8, 16, 1024])
        return pk_ap, scb_ap

    def _decode_one(self, v, i):
        x = self.x
        slot = i % 2
        x.W(v, "vector", "dqa" if slot == 0 else "dqb", self.load_done[i])
        if i >= 2:
            x.W(v, "vector", "dsa" if slot == 0 else "dsb", self.store_done[i - 2])
        pk = self.pk[:, 1024 * slot:1024 * slot + 1024]
        scb = self.scb[:, 1024 * slot:1024 * slot + 1024]
        lo, hi = self.lo[:], self.hi[:]
        b8, mag, t1, t2 = self.b8[:], self.mag[:], self.t1[:], self.t2[:]
        v.tensor_scalar(out=lo, in0=pk, scalar1=15, scalar2=None, op0=ALU.bitwise_and)
        v.tensor_scalar(out=hi, in0=pk, scalar1=4, scalar2=None, op0=ALU.logical_shift_right)
        for pi_, src in ((0, lo), (1, hi)):
            out = self.pl[:, 2048 * slot + 1024 * pi_: 2048 * slot + 1024 * pi_ + 1024]
            v.tensor_scalar(out=b8, in0=src, scalar1=8.0, scalar2=8.0, op0=ALU.is_ge, op1=ALU.mult)
            v.tensor_tensor(out=mag, in0=src, in1=b8, op=ALU.subtract)
            v.tensor_scalar(out=t1, in0=mag, scalar1=0.5, scalar2=None, op0=ALU.mult)
            v.tensor_scalar(out=t2, in0=mag, scalar1=2.0, scalar2=None, op0=ALU.subtract)
            v.tensor_tensor(out=t1, in0=t1, in1=t2, op=ALU.max)
            v.tensor_scalar(out=t2, in0=mag, scalar1=2.0, scalar2=8.0, op0=ALU.mult, op1=ALU.subtract)
            v.tensor_tensor(out=t1, in0=t1, in1=t2, op=ALU.max)
            v.tensor_scalar(out=b8, in0=b8, scalar1=-0.25, scalar2=1.0, op0=ALU.mult, op1=ALU.add)
            v.tensor_tensor(out=t1, in0=t1, in1=b8, op=ALU.mult)
            ins = v.tensor_tensor(out=out, in0=t1, in1=scb, op=ALU.mult)
        self.dec_done[i] = x.I(ins, "dv", 1)

    def _store_one(self, g, i):
        x = self.x
        slot = i % 2
        dss = "dsa" if slot == 0 else "dsb"
        w, k, ncg = self.tiles[i]
        x.W(g, "gpsimd", "dv", self.dec_done[i])
        if w in ("g", "u"):
            dst_t = self.wg_s if w == "g" else self.wu_s
            for pi_, hb in ((0, 2 * k), (1, 2 * k + 1)):
                src = self.pl[:, 2048 * slot + 1024 * pi_: 2048 * slot + 1024 * pi_ + 1024]
                src3 = src.rearrange("p (c n) -> p c n", c=8)
                dst = dst_t[8 * ncg:8 * ncg + 8, :, hb, :].rearrange("c p n -> p c n")
                vv = x.I(g.dma_start(out=dst, in_=src3), dss, 16)
        else:
            j, hh = k, ncg
            for pi_ in (0, 1):
                c = 2 * j + pi_
                src = self.pl[:, 2048 * slot + 1024 * pi_: 2048 * slot + 1024 * pi_ + 1024]
                vv = x.I(g.dma_start(out=self.wd_s[c, :, 1024 * hh:1024 * hh + 1024],
                                     in_=src), dss, 16)
        self.store_done[i] = vv

    def _load_one(self, sync, i):
        x = self.x
        slot = i % 2
        dqs = "dqa" if slot == 0 else "dqb"
        x.W(sync, "sync", "dv",
            max(self.dec_done[i - 2] if i >= 2 else 0, self.start_gate))
        pk_ap, scb_ap = self._tile_srcs(self.tiles[i])
        x.I(sync.dma_start(out=self.pk[:, 1024 * slot:1024 * slot + 1024],
                           in_=pk_ap), dqs, 16)
        v = x.I(sync.dma_start(out=self.scb[:, 1024 * slot:1024 * slot + 1024],
                               in_=scb_ap), dqs, 16)
        self.load_done[i] = v

    def emit(self, block, nsteps):
        done = 0
        while done < nsteps and self.si < self.n:
            t = self.t0
            if 0 <= t - 2 < self.n:
                _body(self.x.block, "vector", lambda v, i=t - 2: self._decode_one(v, i))
                self.di = t - 1
                _body(self.x.block, "gpsimd", lambda g, i=t - 2: self._store_one(g, i))
                self.si = t - 1
            if t < self.n:
                _body(self.x.block, "sync", lambda s, i=t: self._load_one(s, i))
                self.li = t + 1
            self.t0 += 1
            done += 1

    def finish(self, block):
        while self.si < self.n:
            self.emit(block, 8)


def build_program(dev=()):
    nc = bass.Bass("TRN2", target_bir_lowering=False, detect_race_conditions=False)
    x_in = nc.dram_tensor("x", [SEQ, S, HID], F32, kind="ExternalInput").ap()
    xt_in = nc.dram_tensor("xt", [SEQ, HID, S], F32, kind="ExternalInput").ap()
    wqb = nc.dram_tensor("wqb", [NBANDS, 16, 128, 128], F32, kind="ExternalInput").ap()
    wob = nc.dram_tensor("wob", [16, 128, HID], F32, kind="ExternalInput").ap()
    gpk = nc.dram_tensor("gpk", [1024, MLP], I32, kind="ExternalInput").ap()
    upk = nc.dram_tensor("upk", [1024, MLP], I32, kind="ExternalInput").ap()
    dpk = nc.dram_tensor("dpk", [MLP // 2, HID], I32, kind="ExternalInput").ap()
    gsc = nc.dram_tensor("gsc", [64, MLP], F32, kind="ExternalInput").ap()
    usc = nc.dram_tensor("usc", [64, MLP], F32, kind="ExternalInput").ap()
    dsc = nc.dram_tensor("dsc", [MLP // 32, HID], F32, kind="ExternalInput").ap()
    out = nc.dram_tensor("out", [SEQ, S, HID], F32, kind="ExternalOutput").ap()
    wg_s = nc.dram_tensor("wg_s", [NCH, 128, 16, 128], BF16, kind="Internal").ap()
    wu_s = nc.dram_tensor("wu_s", [NCH, 128, 16, 128], BF16, kind="Internal").ap()
    wd_s = nc.dram_tensor("wd_s", [NCH, 128, HID], BF16, kind="Internal").ap()
    q_s = nc.dram_tensor("q_s", [NH, 128, S], BF16, kind="Internal").ap()
    ao_s = nc.dram_tensor("ao_s", [SEQ, NH, 128, S], BF16, kind="Internal").ap()
    xn_s = nc.dram_tensor("xn_s", [SEQ * S, HID], BF16, kind="Internal").ap()
    devt = {}
    for name in dev:
        shp, dt = DEV_SHAPES[name]
        devt[name] = nc.dram_tensor(name, shp, dt, kind="ExternalOutput").ap()

    es = ExitStack()
    with es:
        sb = lambda name, shape, dt: es.enter_context(nc.sbuf_tensor(name, shape, dt))
        cst = sb("cst", [128, 16], F32)
        iden = sb("iden", [128, 128], BF16)
        mask = sb("mask", [128, 128], F32)
        ip = sb("ip", [128, 1], F32)
        invf = sb("invf", [128, 1], F32)
        deq_es = ExitStack()
        sbq = lambda name, shape, dt: deq_es.enter_context(nc.sbuf_tensor(name, shape, dt))
        dq_pk = sbq("dq_pk", [128, 2048], I32)
        dq_sc = sbq("dq_sc", [128, 2048], F32)
        dq_lo = sbq("dq_lo", [128, 1024], I32)
        dq_hi = sbq("dq_hi", [128, 1024], I32)
        dq_b8 = sbq("dq_b8", [128, 1024], BF16)
        dq_mg = sbq("dq_mg", [128, 1024], BF16)
        dq_t1 = sbq("dq_t1", [128, 1024], BF16)
        dq_t2 = sbq("dq_t2", [128, 1024], BF16)
        dq_pl = sbq("dq_pl", [128, 4096], BF16)
        attn_es = ExitStack()
        sba = lambda name, shape, dt: attn_es.enter_context(nc.sbuf_tensor(name, shape, dt))
        tcos = sba("tcos", [128, S], F32)
        tsgn = sba("tsgn", [128, S], F32)
        PS = [es.enter_context(nc.psum_tensor(f"ps{i}", [128, 512], F32)) for i in range(8)]

        block = es.enter_context(nc.Block())
        x = Ctx(nc, block, [
            "dv", "pe", "ac", "p0", "g2", "sw", "qs", "as", "xs", "st",
            "fxa", "fxb", "lba", "lbb", "lqa", "lqb",
            "dqa", "dqb", "dsa", "dsb",
            "wla", "wlb", "p3a", "p3b", "xla", "w4a", "w4b", "wda", "wdb"])
        deq = Dequant(
            x,
            (dq_pk, dq_sc, dq_lo, dq_hi, dq_b8, dq_mg, dq_t1, dq_t2, dq_pl),
            (gpk, upk, dpk, gsc, usc, dsc),
            (wg_s, wu_s, wd_s),
        )

        CST = {"zero": 0, "one": 1, "neg_lninvf": 2, "half_pi": 3,
               "inv2pi": 4, "neg2pi": 5, "cap": 6, "tscale": 7, "eps": 8,
               "invh": 9, "neginvh": 10}

        def p0_g(g):
            x.I(g.iota(dq_lo[:, 0:1], pattern=[[0, 1]], channel_multiplier=1), "p0", 1)
            x.I(g.iota(dq_hi[:, 0:128], pattern=[[1, 128]], channel_multiplier=0), "p0", 1)
            x.I(g.iota(dq_pk[:, 0:S], pattern=[[1, S]], channel_multiplier=0), "p0", 1)
            g.wait_ge(x.sems["p0"], 3)
            x.I(g.tensor_copy(out=ip[:], in_=dq_lo[:, 0:1]), "p0", 1)
            g.wait_ge(x.sems["p0"], 4)
            x.I(g.tensor_scalar(out=iden[:], in0=dq_hi[:, 0:128], scalar1=ip[:],
                                scalar2=None, op0=ALU.is_equal), "p0", 1)
            ins = g.tensor_scalar(out=mask[:], in0=dq_hi[:, 0:128], scalar1=ip[:],
                                  scalar2=-1.0e8, op0=ALU.is_gt, op1=ALU.mult)
            x.I(ins, "p0", 1)
        _body(block, "gpsimd", p0_g)

        def p0_v(v):
            for name, col in CST.items():
                val = {"zero": 0.0, "one": 1.0, "neg_lninvf": -LN1E4 / 64.0,
                       "half_pi": PI / 2.0, "inv2pi": INV2PI, "neg2pi": -TWO_PI,
                       "cap": CAP, "tscale": SCALE / CAP, "eps": EPS,
                       "invh": 1.0 / HID, "neginvh": -1.0 / HID}[name]
                v.memset(cst[:, col:col + 1], val)
            x.W(v, "vector", "p0", 6)
            ins = v.tensor_copy(out=invf[:], in_=ip[:])
            x.I(ins, "dv", 1)
        _body(block, "vector", p0_v)

        def p0_s(s):
            x.W(s, "scalar", "dv", 1)
            x.W(s, "scalar", "p0", 6)
            ins = s.activation(out=invf[0:64, :], in_=ip[0:64, :], func=AF.Exp,
                               bias=cst[0:64, CST["zero"]:CST["zero"] + 1],
                               scale=cst[0:64, CST["neg_lninvf"]:CST["neg_lninvf"] + 1])
            x.I(ins, "ac", 1)
        _body(block, "scalar", p0_s)

        def p0_g2(g):
            x.W(g, "gpsimd", "ac", 1)
            ins = g.dma_start(out=invf[64:128, :], in_=invf[0:64, :])
            x.I(ins, "g2", 16)
        _body(block, "gpsimd", p0_g2)

        dbg = sb("dbg", [128, 4 * S], F32) if "d_p0" in devt else None

        def p0_v2(v):
            x.W(v, "vector", "ac", 1)
            x.W(v, "vector", "g2")
            v.tensor_scalar(out=tcos[:], in0=dq_pk[:, 0:S], scalar1=invf[:], scalar2=None,
                            op0=ALU.mult)
            if dbg is not None:
                v.tensor_copy(out=dbg[:, 0:S], in_=tcos[:])
            tmpf = dq_sc[:, 0:S]
            tmpi = dq_pk[:, 0:S]

            def reduce_(ang):
                v.tensor_scalar(out=tmpf, in0=ang, scalar1=INV2PI, scalar2=None,
                                op0=ALU.mult)
                v.tensor_copy(out=tmpi, in_=tmpf)
                v.tensor_copy(out=tmpf, in_=tmpi)
                v.scalar_tensor_tensor(out=ang, in0=tmpf, scalar=-TWO_PI, in1=ang,
                                       op0=ALU.mult, op1=ALU.add)
                v.tensor_scalar(out=tmpf, in0=ang, scalar1=PI_LO, scalar2=None,
                                op0=ALU.is_gt)
                v.scalar_tensor_tensor(out=ang, in0=tmpf, scalar=-TWO_PI, in1=ang,
                                       op0=ALU.mult, op1=ALU.add)
                return v.tensor_scalar(out=ang, in0=ang, scalar1=PI_LO, scalar2=-PI_LO,
                                       op0=ALU.min, op1=ALU.max)

            v.tensor_copy(out=tsgn[:], in_=tcos[:])
            reduce_(tsgn[:])
            if dbg is not None:
                v.tensor_copy(out=dbg[:, S:2 * S], in_=tsgn[:])
            v.tensor_scalar(out=tcos[:], in0=tcos[:], scalar1=PI / 2.0, scalar2=None,
                            op0=ALU.add)
            ins = reduce_(tcos[:])
            if dbg is not None:
                ins = v.tensor_copy(out=dbg[:, 2 * S:3 * S], in_=tcos[:])
            x.I(ins, "dv", 1)
        _body(block, "vector", p0_v2)

        def p0_s2(s):
            x.W(s, "scalar", "dv")
            one = cst[:, CST["one"]:CST["one"] + 1]
            zero = cst[:, CST["zero"]:CST["zero"] + 1]
            s.activation(out=tsgn[:], in_=tsgn[:], func=AF.Sin, bias=zero, scale=one)
            ins = s.activation(out=tcos[:], in_=tcos[:], func=AF.Sin, bias=zero, scale=one)
            x.I(ins, "ac", 1)
        _body(block, "scalar", p0_s2)

        def p0_v3(v):
            x.W(v, "vector", "ac")
            ins = v.tensor_scalar(out=tsgn[0:64, :], in0=tsgn[0:64, :], scalar1=-1.0,
                                  scalar2=None, op0=ALU.mult)
            x.I(ins, "dv", 1)
        _body(block, "vector", p0_v3)
        trig_done = x.c["dv"]
        deq.start_gate = trig_done

        deq.emit(block, 4)

        if "d_trig" in devt:
            def dvo(g):
                x.W(g, "gpsimd", "dv", trig_done)
                x.I(g.dma_start(out=devt["d_trig"][0:128, :], in_=tcos[:]), "st", 16)
                x.I(g.dma_start(out=devt["d_trig"][128:256, :], in_=tsgn[:]), "st", 16)
                if dbg is not None:
                    x.I(g.dma_start(out=devt["d_p0"][:, :], in_=dbg[:]), "st", 16)
            _body(block, "gpsimd", dvo)

        if "d_wg" in devt:
            deq.finish(block)

            def dvw(g):
                x.W(g, "gpsimd", "dsa")
                x.W(g, "gpsimd", "dsb")
                x.I(g.dma_start(out=devt["d_wg"][:, :, :, :], in_=wg_s[:, :, :, :]), "st", 16)
                x.I(g.dma_start(out=devt["d_wu"][:, :, :, :], in_=wu_s[:, :, :, :]), "st", 16)
                x.I(g.dma_start(out=devt["d_wd"][:, :, :], in_=wd_s[:, :, :]), "st", 16)
                g.wait_ge(x.sems["st"], x.c["st"])
            _body(block, "gpsimd", dvw)
            return nc

        xlf = [sba("xlf0", [128, 1024], F32), sba("xlf1", [128, 1024], F32)]
        ubuf = sba("ubuf", [128, 16 * 1024], BF16)
        xfm = ubuf
        bandf = sba("bandf", [128, 2 * 2048], F32)
        bandb = sba("bandb", [128, 2 * 2048], BF16)
        qsw = sba("qsw", [128, 512], F32)
        qf = sba("qf", [128, 512], F32)
        rtmp = sba("rtmp", [128, 512], F32)
        qstg = sba("qstg", [128, 2 * 512], BF16)
        kfm = sba("kfm", [128, NKV * S], BF16)
        vsb = sba("vsb", [128, NKV * 16 * 128], BF16)
        vtmp = sba("vtmp", [128, 512], BF16)
        qh = sba("qh", [128, 2 * 2048], BF16)
        sn4 = sba("sn4", [128, 4 * 2048], BF16)
        tsb4 = ubuf[:].bitcast(F32)
        sT = sba("sT", [128, 16 * 512], BF16)
        mx4 = sba("mx4", [128, 16], F32)
        aost = sba("aost", [128, 4 * 512], BF16)
        vg = sba("vg", [128, 64], F32)

        def vgap(v):
            v.memset(vg[:, 0:64], 0.0)
            v.memset(vg[:, 0:64], 1.0)

        bank_free = [("dv", 0)] * 8
        pbv = PS[7][:].bitcast(BF16)

        def use_bank(e, ename, i):
            sem, val = bank_free[i]
            x.W(e, ename, sem, val)

        def q_rope_store(b, tc, bank, sglob, s_):
            sl = tc % 2
            pq = PS[bank][:]

            def v_r1(v):
                x.W(v, "vector", "pe")
                x.W(v, "vector", "sw")
                v.tensor_tensor(out=rtmp[:], in0=pq, in1=tcos[:, sglob:sglob + 512], op=ALU.mult)
                ins = v.tensor_copy(out=qf[:], in_=pq)
                x.I(ins, "dv", 1)
            _body(block, "vector", v_r1)
            bank_free[bank] = ("dv", x.c["dv"])

            def g_swap(g):
                x.W(g, "gpsimd", "dv")
                x.I(g.dma_start(out=qsw[0:64, :], in_=qf[64:128, :]), "sw", 16)
                x.I(g.dma_start(out=qsw[64:128, :], in_=qf[0:64, :]), "sw", 16)
            _body(block, "gpsimd", g_swap)

            def v_rope(v):
                x.W(v, "vector", "sw")
                if b < NH:
                    x.W(v, "vector", "qs")
                v.tensor_tensor(out=qsw[:], in0=qsw[:], in1=tsgn[:, sglob:sglob + 512], op=ALU.mult)
                if b < NH:
                    dst = qstg[:, 512 * sl:512 * sl + 512]
                else:
                    dst = kfm[:, (b - NH) * S + sglob:(b - NH) * S + sglob + 512]
                ins = v.tensor_tensor(out=dst, in0=rtmp[:], in1=qsw[:], op=ALU.add)
                x.I(ins, "dv", 1)
            _body(block, "vector", v_rope)

            if b < NH:
                def g_store(g):
                    x.W(g, "gpsimd", "dv")
                    x.W(g, "gpsimd", "lqa")
                    x.W(g, "gpsimd", "lqb")
                    x.I(g.dma_start(out=q_s[b, :, sglob:sglob + 512],
                                    in_=qstg[:, 512 * sl:512 * sl + 512]), "qs", 16)
                _body(block, "gpsimd", g_store)

        def v_evac(b, tc, bank, sglob, s_):
            kv = b - NH - NKV
            def v_cp(v):
                x.W(v, "vector", "pe")
                ins = v.tensor_copy(out=vtmp[:], in_=PS[bank][:])
                x.I(ins, "dv", 1)
            _body(block, "vector", v_cp)
            bank_free[bank] = ("dv", x.c["dv"])
            pslot = (tc % 2) * 512

            def t_tr(t):
                x.W(t, "tensor", "dv")
                use_bank(t, "tensor", 7)
                for j in range(4):
                    ins = t.transpose(pbv[:, pslot + j * 128:pslot + j * 128 + 128],
                                      in_=vtmp[:, j * 128:j * 128 + 128], identity=iden[:])
                x.I(ins, "pe", 1)
            _body(block, "tensor", t_tr)

            kt0 = sglob // 128
            def v_ev(v):
                x.W(v, "vector", "pe")
                ins = v.tensor_copy(out=vsb[:, (kv * 16 + kt0) * 128:(kv * 16 + kt0) * 128 + 512],
                                    in_=pbv[:, pslot:pslot + 512])
                x.I(ins, "dv", 1)
            _body(block, "vector", v_ev)
            bank_free[7] = ("dv", x.c["dv"])

        def p1_seq(s_):
            for th in range(2):
                cast_done = [0, 0]
                for fb in range(16):
                    sl = fb % 2
                    fxs = "fxa" if sl == 0 else "fxb"

                    def s_xl(sy, s_=s_, th=th, fb=fb, sl=sl, fxs=fxs):
                        x.W(sy, "sync", "dv", cast_done[sl])
                        x.I(sy.dma_start(out=xlf[sl][:],
                                         in_=xt_in[s_, fb * 128:fb * 128 + 128,
                                                   th * 1024:th * 1024 + 1024]), fxs, 16)
                    _body(block, "sync", s_xl)

                    def v_cast(v, fb=fb, sl=sl, fxs=fxs):
                        x.W(v, "vector", fxs)
                        x.W(v, "vector", "pe")
                        x.W(v, "vector", "ac")
                        x.W(v, "vector", "st")
                        ins = v.tensor_copy(out=xfm[:, fb * 1024:fb * 1024 + 1024],
                                            in_=xlf[sl][:])
                        x.I(ins, "dv", 1)
                    _body(block, "vector", v_cast)
                    cast_done[sl] = x.c["dv"]
                xfm_done = x.c["dv"]
                if "d_xfm" in devt:
                    def dvxf(g, s_=s_, th=th, xfm_done=xfm_done):
                        x.W(g, "gpsimd", "dv", xfm_done)
                        x.I(g.dma_start(out=devt["d_xfm"][2 * s_ + th], in_=xfm[:, :]), "st", 16)
                    _body(block, "gpsimd", dvxf)

                band_cast_done = [0, 0]
                for b in range(NBANDS):
                    sl = b % 2
                    lbs = "lba" if sl == 0 else "lbb"

                    def s_ld(sy, b=b, sl=sl):
                        x.W(sy, "sync", "pe", band_mm_done.get(sl, 0))
                        x.I(sy.dma_start(out=bandf[:, 2048 * sl:2048 * sl + 2048],
                                         in_=wqb[b].rearrange("hb p m -> p hb m")), lbs, 16)
                    _body(block, "sync", s_ld)

                    def v_cast(v, sl=sl, lbs=lbs):
                        x.W(v, "vector", lbs)
                        x.W(v, "vector", "pe", band_mm_done.get(sl, 0))
                        ins = v.tensor_copy(out=bandb[:, 2048 * sl:2048 * sl + 2048],
                                            in_=bandf[:, 2048 * sl:2048 * sl + 2048])
                        x.I(ins, "dv", 1)
                    _body(block, "vector", v_cast)
                    band_cast_done[sl] = x.c["dv"]

                    for tc in range(2):
                        bank = (b * 2 + tc) % 4
                        sglob = th * 1024 + tc * 512

                        def t_mm(t, sl=sl, bank=bank, tc=tc):
                            x.W(t, "tensor", "dv", max(band_cast_done[sl], xfm_done))
                            use_bank(t, "tensor", bank)
                            for hb in range(16):
                                ins = t.matmul(PS[bank][:],
                                               lhsT=bandb[:, 2048 * sl + hb * 128:2048 * sl + hb * 128 + 128],
                                               rhs=xfm[:, hb * 1024 + tc * 512:hb * 1024 + tc * 512 + 512],
                                               start=(hb == 0), stop=(hb == 15))
                            x.I(ins, "pe", 1)
                        _body(block, "tensor", t_mm)

                        if b < NH + NKV:
                            q_rope_store(b, tc, bank, sglob, s_)
                        else:
                            v_evac(b, tc, bank, sglob, s_)
                    band_mm_done[sl] = x.c["pe"]
                deq.emit(block, 2)

        def p2_seq(s_):
            qh_used = [0, 0]
            kv_ready = x.c["dv"]
            tsb_free = x.c["pe"]
            state = {"tsb_free_dv": 0, "sn_free_pe": 0}
            for hd in range(NH):
                kv = hd // 8
                sl = hd % 2

                lqs = "lqa" if sl == 0 else "lqb"

                def s_qld(sy, hd=hd, sl=sl, lqs=lqs):
                    x.W(sy, "sync", "qs")
                    x.W(sy, "sync", "pe", qh_used[sl])
                    x.I(sy.dma_start(out=qh[:, 2048 * sl:2048 * sl + 2048], in_=q_s[hd]), lqs, 16)
                _body(block, "sync", s_qld)
                qld_here = x.c[lqs]

                for g_ in range(4):
                    for qi in range(4):
                        qt = 4 * g_ + qi
                        kact = (qt + 1) * 128
                        nkc = (kact + 511) // 512

                        def t_sc(t, qi=qi, qt=qt, kact=kact, nkc=nkc, sl=sl,
                                 qld_here=qld_here, lqs=lqs):
                            x.W(t, "tensor", lqs, qld_here)
                            x.W(t, "tensor", "dv", kv_ready)
                            for kc in range(nkc):
                                w = min(512, kact - 512 * kc)
                                use_bank(t, "tensor", kc)
                                ins = t.matmul(PS[kc][:, 0:w],
                                               lhsT=qh[:, 2048 * sl + qt * 128:2048 * sl + qt * 128 + 128],
                                               rhs=kfm[:, kv * S + kc * 512:kv * S + kc * 512 + w],
                                               start=True, stop=True)
                            x.I(ins, "pe", 1)
                        _body(block, "tensor", t_sc)

                        def s_tanh(s, qi=qi, kact=kact, nkc=nkc):
                            x.W(s, "scalar", "pe", tsb_free)
                            x.W(s, "scalar", "pe")
                            x.W(s, "scalar", "dv", state["tsb_free_dv"])
                            x.W(s, "scalar", "st")
                            for kc in range(nkc):
                                w = min(512, kact - 512 * kc)
                                ins = s.activation(out=tsb4[:, qi * 2048 + kc * 512:qi * 2048 + kc * 512 + w],
                                                   in_=PS[kc][:, 0:w], func=AF.Tanh,
                                                   bias=cst[:, CST["zero"]:CST["zero"] + 1],
                                                   scale=cst[:, CST["tscale"]:CST["tscale"] + 1])
                            x.I(ins, "ac", 1)
                        _body(block, "scalar", s_tanh)
                        for kc in range(nkc):
                            bank_free[kc] = ("ac", x.c["ac"])

                    def v_sm(v, g_=g_):
                        x.W(v, "vector", "ac")
                        for qi in range(4):
                            qt = 4 * g_ + qi
                            v.tensor_tensor(out=tsb4[:, qi * 2048 + qt * 128:qi * 2048 + qt * 128 + 128],
                                            in0=tsb4[:, qi * 2048 + qt * 128:qi * 2048 + qt * 128 + 128],
                                            in1=mask[:], op=ALU.add)
                        for qi in range(4):
                            qt = 4 * g_ + qi
                            v.reduce_max(out=mx4[:, qi:qi + 1],
                                         in_=tsb4[:, qi * 2048:qi * 2048 + (qt + 1) * 128],
                                         axis=mybir.AxisListType.X)
                        for qi in range(4):
                            ins = v.tensor_scalar(out=mx4[:, 8 + qi:9 + qi], in0=mx4[:, qi:qi + 1],
                                                  scalar1=-CAP, scalar2=None, op0=ALU.mult)
                        x.I(ins, "dv", 1)
                    _body(block, "vector", v_sm)
                    state["tsb_free_dv"] = x.c["dv"]

                    def s_exp(s, g_=g_):
                        x.W(s, "scalar", "dv")
                        x.W(s, "scalar", "pe", state["sn_free_pe"])
                        for qi in range(4):
                            qt = 4 * g_ + qi
                            kact = (qt + 1) * 128
                            ins = s.activation(out=sn4[:, qi * 2048:qi * 2048 + kact],
                                               in_=tsb4[:, qi * 2048:qi * 2048 + kact], func=AF.Exp,
                                               bias=mx4[:, 8 + qi:9 + qi],
                                               scale=cst[:, CST["cap"]:CST["cap"] + 1],
                                               accum_out=mx4[:, 12 + qi:13 + qi])
                        x.I(ins, "ac", 1)
                    _body(block, "scalar", s_exp)

                    def v_pr(v, g_=g_):
                        x.W(v, "vector", "ac")
                        v.reciprocal(out=mx4[:, 4:8], in_=mx4[:, 12:16])
                        vgap(v)
                        for qi in range(4):
                            qt = 4 * g_ + qi
                            kact = (qt + 1) * 128
                            ins = v.tensor_scalar(out=sn4[:, qi * 2048:qi * 2048 + kact],
                                                  in0=sn4[:, qi * 2048:qi * 2048 + kact],
                                                  scalar1=mx4[:, 4 + qi:5 + qi], scalar2=None,
                                                  op0=ALU.mult)
                        x.I(ins, "dv", 1)
                    _body(block, "vector", v_pr)
                    sn_done = x.c["dv"]

                    for kt in range(4 * g_ + 4):
                        qi0 = max(0, kt - 4 * g_)
                        pslot = (kt % 2) * 512

                        def t_tr(t, kt=kt, qi0=qi0, pslot=pslot, sn_done=sn_done):
                            x.W(t, "tensor", "dv", sn_done)
                            use_bank(t, "tensor", 7)
                            for qi in range(qi0, 4):
                                ins = t.transpose(pbv[:, pslot + qi * 128:pslot + qi * 128 + 128],
                                                  in_=sn4[:, qi * 2048 + kt * 128:qi * 2048 + kt * 128 + 128],
                                                  identity=iden[:])
                            x.I(ins, "pe", 1)
                        _body(block, "tensor", t_tr)

                        def v_ev(v, kt=kt, qi0=qi0, pslot=pslot):
                            x.W(v, "vector", "pe")
                            ins = v.tensor_copy(out=sT[:, kt * 512 + qi0 * 128:kt * 512 + 512],
                                                in_=pbv[:, pslot + qi0 * 128:pslot + 512])
                            x.I(ins, "dv", 1)
                        _body(block, "vector", v_ev)
                        bank_free[7] = ("dv", x.c["dv"])
                    sT_done = x.c["dv"]

                    def t_pv(t, g_=g_, kv=kv, sT_done=sT_done):
                        x.W(t, "tensor", "dv", sT_done)
                        use_bank(t, "tensor", 4)
                        last = 4 * g_ + 3
                        for kt in range(4 * g_ + 4):
                            off = max(0, kt - 4 * g_) * 128
                            ins = t.matmul(PS[4][:, off:512] if off else PS[4][:],
                                           lhsT=vsb[:, (kv * 16 + kt) * 128:(kv * 16 + kt) * 128 + 128],
                                           rhs=sT[:, kt * 512 + off:kt * 512 + 512],
                                           start=(kt == 0), stop=(kt == last))
                        x.I(ins, "pe", 1)
                    _body(block, "tensor", t_pv)
                    state["sn_free_pe"] = x.c["pe"]

                    asl = g_ % 4

                    def v_ao(v, asl=asl):
                        x.W(v, "vector", "pe")
                        x.W(v, "vector", "as")
                        ins = v.tensor_copy(out=aost[:, asl * 512:asl * 512 + 512], in_=PS[4][:])
                        x.I(ins, "dv", 1)
                    _body(block, "vector", v_ao)
                    bank_free[4] = ("dv", x.c["dv"])

                    def g_ao(g, hd=hd, g_=g_, asl=asl, s_=s_):
                        x.W(g, "gpsimd", "dv")
                        x.I(g.dma_start(out=ao_s[s_, hd, :, g_ * 512:g_ * 512 + 512],
                                        in_=aost[:, asl * 512:asl * 512 + 512]), "as", 16)
                    _body(block, "gpsimd", g_ao)
                qh_used[sl] = x.c["pe"]
                deq.emit(block, 8)

        band_mm_done = {}
        for s_ in range(SEQ):
            band_mm_done.clear()
            p1_seq(s_)
            p2_seq(s_)

        if "d_q" in devt:
            def dvq(g):
                x.W(g, "gpsimd", "qs")
                x.W(g, "gpsimd", "as")
                x.W(g, "gpsimd", "dv")
                x.I(g.dma_start(out=devt["d_q"][:, :, :], in_=q_s[:, :, :]), "st", 16)
                x.I(g.dma_start(out=devt["d_k"][0:128, :], in_=kfm[:]), "st", 16)
                x.I(g.dma_start(out=devt["d_v"][0:128, :], in_=vsb[:]), "st", 16)
                x.I(g.dma_start(out=devt["d_ao"][:, :, :, :], in_=ao_s[:, :, :, :]), "st", 16)
                g.wait_ge(x.sems["st"], x.c["st"])
            _body(block, "gpsimd", dvq)

        attn_es.close()
        es3 = ExitStack()
        sb3 = lambda name, shape, dt: es3.enter_context(nc.sbuf_tensor(name, shape, dt))
        wo_sb = sb3("wo_sb", [128, 16 * 2048], BF16)
        wof = sb3("wof", [128, 2 * 2048], F32)
        ao_t = sb3("ao_t", [128, 2 * 2048], BF16)
        xres = sb3("xres", [128, 2 * 2048], F32)
        resb = sb3("resb", [128, 2048], F32)
        sqd = sb3("sqd", [128, 2048], F32)
        xnst = sb3("xnst", [128, 2 * 2048], BF16)
        lnsc = sb3("lnsc", [128, 16], F32)
        vg3 = sb3("vg3", [128, 64], F32)
        FENCE3 = ("pe", "dv", "ac", "st", "qs", "sw", "as")
        fence = {k: x.c[k] for k in FENCE3}

        def p3_seq(s_, first):
            cast_done = [0, 0]
            for fb in range(16):
                sl = fb % 2
                wls = "wla" if sl == 0 else "wlb"

                def s_ld(sy, fb=fb, sl=sl, wls=wls, first=first):
                    if first and fb < 2:
                        for k_ in FENCE3:
                            x.W(sy, "sync", k_, fence[k_])
                    x.W(sy, "sync", "dv", cast_done[sl])
                    x.I(sy.dma_start(out=wof[:, 2048 * sl:2048 * sl + 2048],
                                     in_=wob[fb]), wls, 16)
                _body(block, "sync", s_ld)

                def v_cast(v, fb=fb, sl=sl, wls=wls):
                    x.W(v, "vector", wls)
                    ins = v.tensor_copy(out=wo_sb[:, fb * 2048:fb * 2048 + 2048],
                                        in_=wof[:, 2048 * sl:2048 * sl + 2048])
                    x.I(ins, "dv", 1)
                _body(block, "vector", v_cast)
                cast_done[sl] = x.c["dv"]
            wo_done = x.c["dv"]

            ao_used = [0, 0]
            xres_used = [0, 0]
            resb_free = [0]
            xnst_used = [0, 0]
            for t128 in range(16):
                sl = t128 % 2
                bg = (t128 % 2) * 4
                p3s = "p3a" if sl == 0 else "p3b"

                def g_ld(g, s_=s_, t128=t128, sl=sl, p3s=p3s, first=first):
                    if first and t128 < 2:
                        for k_ in FENCE3:
                            x.W(g, "gpsimd", k_, fence[k_])
                    x.W(g, "gpsimd", "pe", ao_used[sl])
                    x.W(g, "gpsimd", "dv", xres_used[sl])
                    x.W(g, "gpsimd", "as")
                    src = ao_s[s_, :, :, t128 * 128:t128 * 128 + 128].rearrange("f p t -> p f t")
                    x.I(g.dma_start(out=ao_t[:, 2048 * sl:2048 * sl + 2048]
                                    .rearrange("p (f t) -> p f t", f=16), in_=src), p3s, 16)
                    x.I(g.dma_start(out=xres[:, 2048 * sl:2048 * sl + 2048],
                                    in_=x_in[s_, t128 * 128:t128 * 128 + 128, :]), p3s, 16)
                _body(block, "gpsimd", g_ld)
                p3_here = x.c[p3s]

                def t_mm(t, sl=sl, bg=bg, p3s=p3s, p3_here=p3_here, wo_done=wo_done):
                    x.W(t, "tensor", p3s, p3_here)
                    x.W(t, "tensor", "dv", wo_done)
                    for hc in range(4):
                        use_bank(t, "tensor", bg + hc)
                    for ft in range(16):
                        for hc in range(4):
                            ins = t.matmul(PS[bg + hc][:],
                                           lhsT=ao_t[:, 2048 * sl + ft * 128:2048 * sl + ft * 128 + 128],
                                           rhs=wo_sb[:, ft * 2048 + hc * 512:ft * 2048 + hc * 512 + 512],
                                           start=(ft == 0), stop=(ft == 15))
                    x.I(ins, "pe", 1)
                _body(block, "tensor", t_mm)
                ao_used[sl] = x.c["pe"]
                mm_here = x.c["pe"]

                def v_res(v, sl=sl, bg=bg, mm_here=mm_here):
                    x.W(v, "vector", "pe", mm_here)
                    x.W(v, "vector", "ac", resb_free[0])
                    for hc in range(4):
                        ins = v.scalar_tensor_tensor(
                            out=resb[:, hc * 512:hc * 512 + 512], in0=PS[bg + hc][:],
                            scalar=1.0, in1=xres[:, 2048 * sl + hc * 512:2048 * sl + hc * 512 + 512],
                            op0=ALU.mult, op1=ALU.add, accum_out=lnsc[:, hc:hc + 1])
                    x.I(ins, "dv", 1)
                _body(block, "vector", v_res)
                xres_used[sl] = x.c["dv"]
                for hc in range(4):
                    bank_free[bg + hc] = ("dv", x.c["dv"])
                res_here = x.c["dv"]

                def s_sq(s, res_here=res_here):
                    x.W(s, "scalar", "dv", res_here)
                    for hc in range(4):
                        ins = s.activation(out=sqd[:, hc * 512:hc * 512 + 512],
                                           in_=resb[:, hc * 512:hc * 512 + 512], func=AF.Square,
                                           bias=cst[:, CST["zero"]:CST["zero"] + 1],
                                           scale=cst[:, CST["one"]:CST["one"] + 1],
                                           accum_out=lnsc[:, 4 + hc:5 + hc])
                    x.I(ins, "ac", 1)
                _body(block, "scalar", s_sq)
                sq_here = x.c["ac"]
                resb_free[0] = sq_here

                def v_ln(v, sq_here=sq_here):
                    x.W(v, "vector", "ac", sq_here)
                    v.reduce_sum(out=lnsc[:, 8:9], in_=lnsc[:, 0:4], axis=mybir.AxisListType.X)
                    v.reduce_sum(out=lnsc[:, 9:10], in_=lnsc[:, 4:8], axis=mybir.AxisListType.X)
                    v.tensor_scalar(out=lnsc[:, 10:11], in0=lnsc[:, 8:9], scalar1=1.0 / HID,
                                    scalar2=None, op0=ALU.mult)
                    v.tensor_scalar(out=lnsc[:, 11:12], in0=lnsc[:, 9:10], scalar1=1.0 / HID,
                                    scalar2=None, op0=ALU.mult)
                    v.tensor_scalar(out=lnsc[:, 12:13], in0=lnsc[:, 10:11], scalar1=-1.0,
                                    scalar2=None, op0=ALU.mult)
                    v.tensor_tensor(out=lnsc[:, 13:14], in0=lnsc[:, 10:11], in1=lnsc[:, 10:11],
                                    op=ALU.mult)
                    v.memset(vg3[:, 0:64], 0.0)
                    v.memset(vg3[:, 0:64], 1.0)
                    ins = v.tensor_tensor(out=lnsc[:, 14:15], in0=lnsc[:, 11:12],
                                          in1=lnsc[:, 13:14], op=ALU.subtract)
                    x.I(ins, "dv", 1)
                _body(block, "vector", v_ln)
                ln_here = x.c["dv"]

                def s_rstd(s, ln_here=ln_here):
                    x.W(s, "scalar", "dv", ln_here)
                    ins = s.activation(out=lnsc[:, 15:16], in_=lnsc[:, 14:15], func=AF.Sqrt,
                                       bias=cst[:, CST["eps"]:CST["eps"] + 1],
                                       scale=cst[:, CST["one"]:CST["one"] + 1])
                    x.I(ins, "ac", 1)
                _body(block, "scalar", s_rstd)
                rstd_here = x.c["ac"]

                def v_xn(v, sl=sl, rstd_here=rstd_here):
                    x.W(v, "vector", "ac", rstd_here)
                    x.W(v, "vector", "xs")
                    v.reciprocal(out=lnsc[:, 7:8], in_=lnsc[:, 15:16])
                    v.memset(vg3[:, 0:64], 0.0)
                    v.memset(vg3[:, 0:64], 1.0)
                    ins = v.tensor_scalar(out=xnst[:, 2048 * sl:2048 * sl + 2048], in0=resb[:],
                                          scalar1=lnsc[:, 12:13], scalar2=lnsc[:, 7:8],
                                          op0=ALU.add, op1=ALU.mult)
                    x.I(ins, "dv", 1)
                _body(block, "vector", v_xn)

                def g_st(g, s_=s_, t128=t128, sl=sl):
                    x.W(g, "gpsimd", "dv")
                    x.I(g.dma_start(out=xn_s[s_ * S + t128 * 128:s_ * S + t128 * 128 + 128, :],
                                    in_=xnst[:, 2048 * sl:2048 * sl + 2048]), "xs", 16)
                _body(block, "gpsimd", g_st)
                xnst_used[sl] = x.c["xs"]
                if t128 % 4 == 3:
                    deq.emit(block, 2)

        p3_seq(0, True)
        p3_seq(1, False)

        if "d_xn" in devt:
            def dvx(g):
                x.W(g, "gpsimd", "xs")
                x.I(g.dma_start(out=devt["d_xn"][:, :], in_=xn_s[:, :]), "st", 16)
            _body(block, "gpsimd", dvx)
        if devt and "d_wg" not in devt and "stop_p3" in devt:
            es3.close()
            deq_es.close()
            return nc

        deq.finish(block)
        es3.close()
        deq_es.close()

        es4 = ExitStack()
        sb4 = lambda name, shape, dt: es4.enter_context(nc.sbuf_tensor(name, shape, dt))
        acc = sb4("acc", [128, 8 * 2048], F32)
        xfm4 = sb4("xfm4", [128, 16 * TB], BF16)
        inter = sb4("inter", [128, 16 * TB], BF16)
        wgc = sb4("wgc", [128, 2 * 2048], BF16)
        wuc = sb4("wuc", [128, 2 * 2048], BF16)
        wds = sb4("wds", [128, 2 * 8192], BF16)
        xnt4 = sb4("xnt4", [128, 2048], BF16)
        gl4 = sb4("gl4", [128, 512], F32)
        res4 = sb4("res4", [128, 2048], F32)
        sq4 = sb4("sq4", [128, 2048], F32)
        xo4 = sb4("xo4", [128, 2048], F32)
        ln4 = sb4("ln4", [128, 16], F32)
        vg4 = sb4("vg4", [128, 64], F32)
        FENCE4 = ("pe", "dv", "ac", "st", "xs", "dsa", "dsb", "qs", "sw", "as")
        fence4 = {k: x.c[k] for k in FENCE4}

        s1_done_prev = [0]
        s2_done_prev = [0]
        xfm4_free = [0]
        state4 = {"xnt_used": 0, "xnt_used_dv": 0, "wg_used": [0, 0], "res_free": 0,
                  "res_free2": 0, "xo_used": 0, "wds_used": [0, 0]}

        for tb in range(4):
            for tt_ in range(8):
                gt0 = tb * TB + tt_ * 128

                def g_xl(g, gt0=gt0, tb=tb, tt_=tt_):
                    if tb == 0 and tt_ == 0:
                        for k_ in FENCE4:
                            x.W(g, "gpsimd", k_, fence4[k_])
                    x.W(g, "gpsimd", "pe", state4["xnt_used"])
                    x.W(g, "gpsimd", "dv", state4["xnt_used_dv"])
                    x.W(g, "gpsimd", "xs")
                    x.I(g.dma_start(out=xnt4[:], in_=xn_s[gt0:gt0 + 128, :]), "xla", 16)
                _body(block, "gpsimd", g_xl)
                g2h = x.c["xla"]

                for grp in range(4):
                    pslot = (grp % 2) * 512

                    def t_tr(t, grp=grp, pslot=pslot, g2h=g2h, tb=tb, tt_=tt_):
                        x.W(t, "tensor", "xla", g2h)
                        x.W(t, "tensor", "pe", s1_done_prev[0])
                        use_bank(t, "tensor", 7)
                        for j in range(4):
                            fb = grp * 4 + j
                            k_ = fb // 2
                            e_ = fb % 2
                            ins = t.transpose(pbv[:, pslot + j * 128:pslot + j * 128 + 128],
                                              in_=xnt4[:, 256 * k_ + e_:256 * k_ + 256:2],
                                              identity=iden[:])
                        x.I(ins, "pe", 1)
                    _body(block, "tensor", t_tr)

                    def v_ev(v, grp=grp, pslot=pslot, tt_=tt_):
                        x.W(v, "vector", "pe")
                        dst = xfm4[:, :].rearrange("p (f t) -> p f t", f=16)[
                            :, grp * 4:grp * 4 + 4, tt_ * 128:tt_ * 128 + 128]
                        ins = v.tensor_copy(out=dst, in_=pbv[:, pslot:pslot + 512]
                                            .rearrange("p (f t) -> p f t", f=4))
                        x.I(ins, "dv", 1)
                    _body(block, "vector", v_ev)
                    bank_free[7] = ("dv", x.c["dv"])
                state4["xnt_used"] = x.c["pe"]
            xfm4_done = x.c["dv"]

            def v_z(v):
                ins = v.memset(acc[:], 0.0)
                x.I(ins, "dv", 1)
            _body(block, "vector", v_z)

            for sec in range(NSEC):
                for ci in range(SECC):
                    c = sec * SECC + ci
                    slw = ci % 2

                    w4s = "w4a" if slw == 0 else "w4b"

                    def s_wl(sy, c=c, slw=slw, w4s=w4s, tb=tb, sec=sec, ci=ci):
                        if tb == 0 and sec == 0 and ci < 2:
                            for k_ in FENCE4:
                                x.W(sy, "sync", k_, fence4[k_])
                        x.W(sy, "sync", "dsa")
                        x.W(sy, "sync", "dsb")
                        x.W(sy, "sync", "pe", state4["wg_used"][slw])
                        x.I(sy.dma_start(out=wgc[:, 2048 * slw:2048 * slw + 2048],
                                         in_=wg_s[c].rearrange("p h n -> p (h n)")), w4s, 16)
                        x.I(sy.dma_start(out=wuc[:, 2048 * slw:2048 * slw + 2048],
                                         in_=wu_s[c].rearrange("p h n -> p (h n)")), w4s, 16)
                    _body(block, "sync", s_wl)
                    ldh = x.c[w4s]

                    bg = (ci % 2) * 2

                    for half in range(2):
                        def t_h(t, ci=ci, slw=slw, bg=bg, half=half, w4s=w4s, ldh=ldh,
                                xfm4_done=xfm4_done):
                            x.W(t, "tensor", w4s, ldh)
                            x.W(t, "tensor", "dv", xfm4_done)
                            use_bank(t, "tensor", bg)
                            use_bank(t, "tensor", bg + 1)
                            for hb in range(16):
                                t.matmul(PS[bg][:],
                                         lhsT=wgc[:, 2048 * slw + hb * 128:2048 * slw + hb * 128 + 128],
                                         rhs=xfm4[:, hb * TB + half * 512:hb * TB + half * 512 + 512],
                                         start=(hb == 0), stop=(hb == 15))
                            for hb in range(16):
                                ins = t.matmul(PS[bg + 1][:],
                                               lhsT=wuc[:, 2048 * slw + hb * 128:2048 * slw + hb * 128 + 128],
                                               rhs=xfm4[:, hb * TB + half * 512:hb * TB + half * 512 + 512],
                                               start=(hb == 0), stop=(hb == 15))
                            x.I(ins, "pe", 1)
                        _body(block, "tensor", t_h)
                        peh = x.c["pe"]

                        def s_gel(sc, bg=bg, peh=peh):
                            x.W(sc, "scalar", "pe", peh)
                            x.W(sc, "scalar", "dv", state4["res_free"])
                            ins2 = sc.activation(out=gl4[:], in_=PS[bg][:],
                                                 func=AF.Gelu_apprx_tanh,
                                                 bias=cst[:, CST["zero"]:CST["zero"] + 1],
                                                 scale=cst[:, CST["one"]:CST["one"] + 1])
                            x.I(ins2, "ac", 1)
                        _body(block, "scalar", s_gel)
                        bank_free[bg] = ("ac", x.c["ac"])

                        def v_int(v, ci=ci, bg=bg, half=half):
                            x.W(v, "vector", "ac")
                            x.W(v, "vector", "pe", s2_done_prev[0])
                            ins2 = v.tensor_tensor(
                                out=inter[:, ci * TB + half * 512:ci * TB + half * 512 + 512],
                                in0=gl4[:], in1=PS[bg + 1][:], op=ALU.mult)
                            x.I(ins2, "dv", 1)
                        _body(block, "vector", v_int)
                        bank_free[bg + 1] = ("dv", x.c["dv"])
                        state4["res_free"] = x.c["dv"]
                    state4["wg_used"][slw] = x.c["pe"]
                s1_done = x.c["dv"]
                if tb == 0 and sec == 0 and "d_int" in devt:
                    def dvi(g):
                        x.W(g, "gpsimd", "dv", s1_done)
                        x.I(g.dma_start(out=devt["d_int"][:, :], in_=inter[:]), "st", 16)
                    _body(block, "gpsimd", dvi)

                for hq in range(4):
                    slw = hq % 2

                    wdsem = "wda" if slw == 0 else "wdb"

                    def s_wd(sy, sec=sec, hq=hq, slw=slw, wdsem=wdsem, tb=tb):
                        if tb == 0 and sec == 0 and hq < 2:
                            for k_ in FENCE4:
                                x.W(sy, "sync", k_, fence4[k_])
                        x.W(sy, "sync", "dsa")
                        x.W(sy, "sync", "dsb")
                        x.W(sy, "sync", "pe", state4["wds_used"][slw])
                        src = wd_s[sec * SECC:sec * SECC + SECC, :, hq * 512:hq * 512 + 512]
                        x.I(sy.dma_start(out=wds[:, 8192 * slw:8192 * slw + 8192]
                                         .rearrange("p (c h) -> p c h", c=16),
                                         in_=src.rearrange("c p h -> p c h")), wdsem, 16)
                    _body(block, "sync", s_wd)
                    ldh2 = x.c[wdsem]

                    for t128 in range(8):
                        def t_s2(t, hq=hq, slw=slw, t128=t128, wdsem=wdsem, ldh2=ldh2,
                                 s1_done=s1_done):
                            x.W(t, "tensor", wdsem, ldh2)
                            x.W(t, "tensor", "dv", s1_done)
                            use_bank(t, "tensor", t128)
                            for ci in range(SECC):
                                ins = t.matmul(PS[t128][:],
                                               lhsT=inter[:, ci * TB + t128 * 128:ci * TB + t128 * 128 + 128],
                                               rhs=wds[:, 8192 * slw + ci * 512:8192 * slw + ci * 512 + 512],
                                               start=(ci == 0), stop=(ci == SECC - 1))
                            x.I(ins, "pe", 1)
                        _body(block, "tensor", t_s2)

                        def v_acc(v, hq=hq, t128=t128):
                            x.W(v, "vector", "pe")
                            dst = acc[:, t128 * 2048 + hq * 512:t128 * 2048 + hq * 512 + 512]
                            ins = v.tensor_tensor(out=dst, in0=dst, in1=PS[t128][:], op=ALU.add)
                            x.I(ins, "dv", 1)
                        _body(block, "vector", v_acc)
                        bank_free[t128] = ("dv", x.c["dv"])
                    state4["wds_used"][slw] = x.c["pe"]
                s2_done_prev[0] = x.c["pe"]
            acc_done = x.c["dv"]
            if tb == 0 and "d_acc" in devt:
                def dva(g):
                    x.W(g, "gpsimd", "dv", acc_done)
                    x.I(g.dma_start(out=devt["d_acc"][:, :], in_=acc[:]), "st", 16)
                _body(block, "gpsimd", dva)

            for tt_ in range(8):
                gt0 = tb * TB + tt_ * 128
                s_ = gt0 // S
                row = gt0 % S

                def g_xl2(g, gt0=gt0):
                    x.W(g, "gpsimd", "pe", state4["xnt_used"])
                    x.W(g, "gpsimd", "dv", state4["xnt_used_dv"])
                    x.I(g.dma_start(out=xnt4[:], in_=xn_s[gt0:gt0 + 128, :]), "xla", 16)
                _body(block, "gpsimd", g_xl2)
                g2h2 = x.c["xla"]

                def v_r2(v, g2h2=g2h2, acc_done=acc_done, tt_=tt_):
                    x.W(v, "vector", "xla", g2h2)
                    x.W(v, "vector", "dv", acc_done)
                    x.W(v, "vector", "ac", state4["res_free2"])
                    ins = v.scalar_tensor_tensor(out=res4[:], in0=acc[:, tt_ * 2048:tt_ * 2048 + 2048],
                                                 scalar=1.0, in1=xnt4[:],
                                                 op0=ALU.mult, op1=ALU.add,
                                                 accum_out=ln4[:, 0:1])
                    x.I(ins, "dv", 1)
                _body(block, "vector", v_r2)
                state4["xnt_used_dv"] = x.c["dv"]
                rh = x.c["dv"]

                def s_sq2(sc, rh=rh):
                    x.W(sc, "scalar", "dv", rh)
                    ins = sc.activation(out=sq4[:], in_=res4[:], func=AF.Square,
                                        bias=cst[:, CST["zero"]:CST["zero"] + 1],
                                        scale=cst[:, CST["one"]:CST["one"] + 1],
                                        accum_out=ln4[:, 1:2])
                    x.I(ins, "ac", 1)
                _body(block, "scalar", s_sq2)
                state4["res_free2"] = x.c["ac"]
                sqh = x.c["ac"]

                def v_ln2(v, sqh=sqh):
                    x.W(v, "vector", "ac", sqh)
                    v.tensor_scalar(out=ln4[:, 2:3], in0=ln4[:, 0:1], scalar1=1.0 / HID,
                                    scalar2=None, op0=ALU.mult)
                    v.tensor_scalar(out=ln4[:, 3:4], in0=ln4[:, 1:2], scalar1=1.0 / HID,
                                    scalar2=None, op0=ALU.mult)
                    v.tensor_scalar(out=ln4[:, 4:5], in0=ln4[:, 2:3], scalar1=-1.0,
                                    scalar2=None, op0=ALU.mult)
                    v.tensor_tensor(out=ln4[:, 5:6], in0=ln4[:, 2:3], in1=ln4[:, 2:3], op=ALU.mult)
                    v.memset(vg4[:, 0:64], 0.0)
                    v.memset(vg4[:, 0:64], 1.0)
                    ins = v.tensor_tensor(out=ln4[:, 6:7], in0=ln4[:, 3:4], in1=ln4[:, 5:6],
                                          op=ALU.subtract)
                    x.I(ins, "dv", 1)
                _body(block, "vector", v_ln2)
                lh = x.c["dv"]

                def s_sd(sc, lh=lh):
                    x.W(sc, "scalar", "dv", lh)
                    ins = sc.activation(out=ln4[:, 7:8], in_=ln4[:, 6:7], func=AF.Sqrt,
                                        bias=cst[:, CST["eps"]:CST["eps"] + 1],
                                        scale=cst[:, CST["one"]:CST["one"] + 1])
                    x.I(ins, "ac", 1)
                _body(block, "scalar", s_sd)
                sdh = x.c["ac"]

                def v_out(v, sdh=sdh):
                    x.W(v, "vector", "ac", sdh)
                    x.W(v, "vector", "st")
                    v.reciprocal(out=ln4[:, 8:9], in_=ln4[:, 7:8])
                    v.memset(vg4[:, 0:64], 0.0)
                    v.memset(vg4[:, 0:64], 1.0)
                    ins = v.tensor_scalar(out=xo4[:], in0=res4[:],
                                          scalar1=ln4[:, 4:5], scalar2=ln4[:, 8:9],
                                          op0=ALU.add, op1=ALU.mult)
                    x.I(ins, "dv", 1)
                _body(block, "vector", v_out)
                oh = x.c["dv"]

                def g_out(g, s_=s_, row=row, oh=oh):
                    x.W(g, "gpsimd", "dv", oh)
                    x.I(g.dma_start(out=out[s_, row:row + 128, :], in_=xo4[:]), "st", 16)
                _body(block, "gpsimd", g_out)
                state4["xo_used"] = x.c["st"]

        es4.close()

        def fin(g):
            g.wait_ge(x.sems["st"], x.c["st"])
        _body(block, "gpsimd", fin)

    return nc


DEV_SHAPES = {
    "d_xfm": ([4, 128, 16 * 1024], BF16),
    "d_trig": ([256, S], F32),
    "d_p0": ([128, 4 * S], F32),
    "d_q": ([NH, 128, S], BF16),
    "d_k": ([128, NKV * S], BF16),
    "d_v": ([128, NKV * 16 * 128], BF16),
    "d_ao": ([SEQ, NH, 128, S], BF16),
    "d_xn": ([SEQ * S, HID], BF16),
    "d_int": ([128, 16 * TB], BF16),
    "d_acc": ([128, 8 * 2048], F32),
    "d_wg": ([NCH, 128, 16, 128], BF16),
    "d_wu": ([NCH, 128, 16, 128], BF16),
    "d_wd": ([NCH, 128, HID], BF16),
}


def _perm16384():
    return np.arange(MLP).reshape(64, 128, 2).transpose(0, 2, 1).reshape(-1)


def prep_inputs(L0_attn_hidden_states, L0_attn_Wqkv, L0_attn_Wo,
                L0_ffn_gate_W_packed, L0_ffn_gate_scales,
                L0_ffn_up_W_packed, L0_ffn_up_scales,
                L0_ffn_down_W_packed, L0_ffn_down_scales):
    P = _perm16384()
    wqb = np.ascontiguousarray(
        np.asarray(L0_attn_Wqkv, dtype=np.float32)
        .reshape(16, 128, NBANDS, 128).transpose(2, 0, 1, 3))
    wob = np.ascontiguousarray(np.asarray(L0_attn_Wo, dtype=np.float32).reshape(16, 128, HID))
    gpk_ = np.ascontiguousarray(np.asarray(L0_ffn_gate_W_packed, dtype=np.int32).T[:, P])
    upk_ = np.ascontiguousarray(np.asarray(L0_ffn_up_W_packed, dtype=np.int32).T[:, P])
    dpk_ = np.ascontiguousarray(np.asarray(L0_ffn_down_W_packed, dtype=np.int32).T)
    gsc_ = np.ascontiguousarray(np.asarray(L0_ffn_gate_scales, dtype=np.float32).T[:, P])
    usc_ = np.ascontiguousarray(np.asarray(L0_ffn_up_scales, dtype=np.float32).T[:, P])
    dsc_ = np.ascontiguousarray(np.asarray(L0_ffn_down_scales, dtype=np.float32).T)
    xf = np.asarray(L0_attn_hidden_states, dtype=np.float32)
    in_maps = []
    for c in range(8):
        xs_ = np.ascontiguousarray(xf[SEQ * c:SEQ * c + SEQ])
        in_maps.append({
            "x": xs_,
            "xt": np.ascontiguousarray(xs_.transpose(0, 2, 1)),
            "wqb": wqb, "wob": wob,
            "gpk": gpk_, "upk": upk_, "dpk": dpk_,
            "gsc": gsc_, "usc": usc_, "dsc": dsc_,
        })
    return in_maps


def kernel(**inputs):
    in_maps = prep_inputs(**inputs)
    nc = build_program(dev=DEV)
    res = run_bass_kernel_spmd(nc, in_maps, core_ids=list(range(8)))
    outs = [r["out"] for r in res.results]
    return np.concatenate(outs, axis=0)


# version 05 (final)
# speedup vs baseline: 1.0931x; 1.0931x over previous
import numpy as np
from contextlib import ExitStack

import concourse.bass as bass
import concourse.mybir as mybir
from concourse.bass_utils import run_bass_kernel_spmd

F32 = mybir.dt.float32
BF16 = mybir.dt.bfloat16
I32 = mybir.dt.int32
AF = mybir.ActivationFunctionType
ALU = mybir.AluOpType

SEQ = 2
S = 2048
HID = 2048
NH, NKV, DH = 16, 2, 128
NBANDS = 20
MLP = 16384
NCH = 128
SCALE = 0.08838834764831845
CAP = 30.0
EPS = 1e-5
PI = float(np.pi)
PI_LO = float(np.nextafter(np.float32(np.pi), np.float32(0.0)))
INV2PI = float(1.0 / (2.0 * np.pi))
TWO_PI = float(2.0 * np.pi)
LN1E4 = float(np.log(10000.0))

TB = 1024
NSEC = 8
SECC = 16

DEV = ()


def _body(block, eng, fn):
    getattr(block, eng)(fn)


class Ctx:
    def __init__(self, nc, block, sem_names):
        self.nc = nc
        self.block = block
        self.sems = {}
        self.c = {}
        self._es = ExitStack()
        for n in sem_names:
            self.sems[n] = self._es.enter_context(nc.semaphore(n))
            self.c[n] = 0
        self._last = {}

    def W(self, e, ename, sem, val=None):
        if val is None:
            val = self.c[sem]
        if val <= 0:
            return
        key = (ename, sem)
        if self._last.get(key, 0) >= val:
            return
        e.wait_ge(self.sems[sem], val)
        self._last[key] = val

    def I(self, ins, sem, n=1):
        self.c[sem] += n
        ins.then_inc(self.sems[sem], n)
        return self.c[sem]


class Dequant:

    def __init__(self, ctx, bufs, srcs, dsts):
        self.x = ctx
        (self.pk, self.scb, self.lo, self.hi, self.b8, self.mag,
         self.t1, self.t2, self.pl) = bufs
        self.gpk, self.upk, self.dpk, self.gsc, self.usc, self.dsc = srcs
        self.wg_s, self.wu_s, self.wd_s = dsts
        self.start_gate = 0
        self.tiles = []
        for ncg in range(16):
            for w in ("g", "u"):
                for k in range(8):
                    self.tiles.append((w, k, ncg))
            for j in range(4 * ncg, 4 * ncg + 4):
                for hh in range(2):
                    self.tiles.append(("d", j, hh))
        self.n = len(self.tiles)
        self.t0 = 0
        self.li = 0
        self.di = 0
        self.si = 0
        self.load_done = [0] * self.n
        self.dec_done = [0] * self.n
        self.store_done = [0] * self.n
        self.chunk_ready = [0] * NCH

    def _tile_srcs(self, t):
        w, k, ncg = t
        if w in ("g", "u"):
            pk = self.gpk if w == "g" else self.upk
            sc = self.gsc if w == "g" else self.usc
            pk_ap = pk[128 * k:128 * k + 128, 1024 * ncg:1024 * ncg + 1024]
            sc_ap = sc[8 * k:8 * k + 8, 1024 * ncg:1024 * ncg + 1024]
        else:
            j, hh = k, ncg
            pk_ap = self.dpk[128 * j:128 * j + 128, 1024 * hh:1024 * hh + 1024]
            sc_ap = self.dsc[8 * j:8 * j + 8, 1024 * hh:1024 * hh + 1024]
        scb_ap = sc_ap.unsqueeze(1).broadcast_to([8, 16, 1024])
        return pk_ap, scb_ap

    def _decode_one(self, v, i):
        x = self.x
        slot = i % 2
        x.W(v, "vector", "dqa" if slot == 0 else "dqb", self.load_done[i])
        if i >= 2:
            x.W(v, "vector", "dsa" if slot == 0 else "dsb", self.store_done[i - 2])
        pk = self.pk[:, 1024 * slot:1024 * slot + 1024]
        scb = self.scb[:, 1024 * slot:1024 * slot + 1024]
        lo, hi = self.lo[:], self.hi[:]
        b8, mag, t1, t2 = self.b8[:], self.mag[:], self.t1[:], self.t2[:]
        v.tensor_scalar(out=lo, in0=pk, scalar1=15, scalar2=None, op0=ALU.bitwise_and)
        v.tensor_scalar(out=hi, in0=pk, scalar1=4, scalar2=None, op0=ALU.logical_shift_right)
        for pi_, src in ((0, lo), (1, hi)):
            out = self.pl[:, 2048 * slot + 1024 * pi_: 2048 * slot + 1024 * pi_ + 1024]
            v.tensor_scalar(out=b8, in0=src, scalar1=8.0, scalar2=8.0, op0=ALU.is_ge, op1=ALU.mult)
            v.tensor_tensor(out=mag, in0=src, in1=b8, op=ALU.subtract)
            v.tensor_scalar(out=t1, in0=mag, scalar1=0.5, scalar2=None, op0=ALU.mult)
            v.tensor_scalar(out=t2, in0=mag, scalar1=2.0, scalar2=None, op0=ALU.subtract)
            v.tensor_tensor(out=t1, in0=t1, in1=t2, op=ALU.max)
            v.tensor_scalar(out=t2, in0=mag, scalar1=2.0, scalar2=8.0, op0=ALU.mult, op1=ALU.subtract)
            v.tensor_tensor(out=t1, in0=t1, in1=t2, op=ALU.max)
            v.tensor_scalar(out=b8, in0=b8, scalar1=-0.25, scalar2=1.0, op0=ALU.mult, op1=ALU.add)
            v.tensor_tensor(out=t1, in0=t1, in1=b8, op=ALU.mult)
            ins = v.tensor_tensor(out=out, in0=t1, in1=scb, op=ALU.mult)
        self.dec_done[i] = x.I(ins, "dv", 1)

    def _store_one(self, g, i):
        x = self.x
        slot = i % 2
        dss = "dsa" if slot == 0 else "dsb"
        w, k, ncg = self.tiles[i]
        x.W(g, "gpsimd", "dv", self.dec_done[i])
        if w in ("g", "u"):
            dst_t = self.wg_s if w == "g" else self.wu_s
            for pi_, hb in ((0, 2 * k), (1, 2 * k + 1)):
                src = self.pl[:, 2048 * slot + 1024 * pi_: 2048 * slot + 1024 * pi_ + 1024]
                src3 = src.rearrange("p (c n) -> p c n", c=8)
                dst = dst_t[8 * ncg:8 * ncg + 8, :, hb, :].rearrange("c p n -> p c n")
                vv = x.I(g.dma_start(out=dst, in_=src3), dss, 16)
        else:
            j, hh = k, ncg
            for pi_ in (0, 1):
                c = 2 * j + pi_
                src = self.pl[:, 2048 * slot + 1024 * pi_: 2048 * slot + 1024 * pi_ + 1024]
                vv = x.I(g.dma_start(out=self.wd_s[c, :, 1024 * hh:1024 * hh + 1024],
                                     in_=src), dss, 16)
        self.store_done[i] = vv

    def _load_one(self, sync, i):
        x = self.x
        slot = i % 2
        dqs = "dqa" if slot == 0 else "dqb"
        x.W(sync, "sync", "dv",
            max(self.dec_done[i - 2] if i >= 2 else 0, self.start_gate))
        pk_ap, scb_ap = self._tile_srcs(self.tiles[i])
        x.I(sync.dma_start(out=self.pk[:, 1024 * slot:1024 * slot + 1024],
                           in_=pk_ap), dqs, 16)
        v = x.I(sync.dma_start(out=self.scb[:, 1024 * slot:1024 * slot + 1024],
                               in_=scb_ap), dqs, 16)
        self.load_done[i] = v

    def emit(self, block, nsteps):
        done = 0
        while done < nsteps and self.si < self.n:
            t = self.t0
            if 0 <= t - 2 < self.n:
                _body(self.x.block, "vector", lambda v, i=t - 2: self._decode_one(v, i))
                self.di = t - 1
                _body(self.x.block, "gpsimd", lambda g, i=t - 2: self._store_one(g, i))
                self.si = t - 1
            if t < self.n:
                _body(self.x.block, "sync", lambda s, i=t: self._load_one(s, i))
                self.li = t + 1
            self.t0 += 1
            done += 1

    def finish(self, block):
        while self.si < self.n:
            self.emit(block, 8)


def build_program(dev=()):
    nc = bass.Bass("TRN2", target_bir_lowering=False, detect_race_conditions=False)
    x_in = nc.dram_tensor("x", [SEQ, S, HID], F32, kind="ExternalInput").ap()
    xt_in = nc.dram_tensor("xt", [SEQ, HID, S], F32, kind="ExternalInput").ap()
    wqb = nc.dram_tensor("wqb", [NBANDS, 16, 128, 128], F32, kind="ExternalInput").ap()
    wob = nc.dram_tensor("wob", [16, 128, HID], F32, kind="ExternalInput").ap()
    gpk = nc.dram_tensor("gpk", [1024, MLP], I32, kind="ExternalInput").ap()
    upk = nc.dram_tensor("upk", [1024, MLP], I32, kind="ExternalInput").ap()
    dpk = nc.dram_tensor("dpk", [MLP // 2, HID], I32, kind="ExternalInput").ap()
    gsc = nc.dram_tensor("gsc", [64, MLP], F32, kind="ExternalInput").ap()
    usc = nc.dram_tensor("usc", [64, MLP], F32, kind="ExternalInput").ap()
    dsc = nc.dram_tensor("dsc", [MLP // 32, HID], F32, kind="ExternalInput").ap()
    out = nc.dram_tensor("out", [SEQ, S, HID], F32, kind="ExternalOutput").ap()
    wg_s = nc.dram_tensor("wg_s", [NCH, 128, 16, 128], BF16, kind="Internal").ap()
    wu_s = nc.dram_tensor("wu_s", [NCH, 128, 16, 128], BF16, kind="Internal").ap()
    wd_s = nc.dram_tensor("wd_s", [NCH, 128, HID], BF16, kind="Internal").ap()
    q_s = nc.dram_tensor("q_s", [NH, 128, S], BF16, kind="Internal").ap()
    dn_s = nc.dram_tensor("dn_s", [2, 4, 128], F32, kind="Internal").ap()
    ao_s = nc.dram_tensor("ao_s", [SEQ, NH, 128, S], BF16, kind="Internal").ap()
    xn_s = nc.dram_tensor("xn_s", [SEQ * S, HID], BF16, kind="Internal").ap()
    devt = {}
    for name in dev:
        shp, dt = DEV_SHAPES[name]
        devt[name] = nc.dram_tensor(name, shp, dt, kind="ExternalOutput").ap()

    es = ExitStack()
    with es:
        sb = lambda name, shape, dt: es.enter_context(nc.sbuf_tensor(name, shape, dt))
        cst = sb("cst", [128, 16], F32)
        iden = sb("iden", [128, 128], BF16)
        idenf = sb("idenf", [128, 128], F32)
        mask = sb("mask", [128, 128], F32)
        ip = sb("ip", [128, 1], F32)
        invf = sb("invf", [128, 1], F32)
        deq_es = ExitStack()
        sbq = lambda name, shape, dt: deq_es.enter_context(nc.sbuf_tensor(name, shape, dt))
        dq_pk = sbq("dq_pk", [128, 2048], I32)
        dq_sc = sbq("dq_sc", [128, 2048], F32)
        dq_lo = sbq("dq_lo", [128, 1024], I32)
        dq_hi = sbq("dq_hi", [128, 1024], I32)
        dq_b8 = sbq("dq_b8", [128, 1024], BF16)
        dq_mg = sbq("dq_mg", [128, 1024], BF16)
        dq_t1 = sbq("dq_t1", [128, 1024], BF16)
        dq_t2 = sbq("dq_t2", [128, 1024], BF16)
        dq_pl = sbq("dq_pl", [128, 4096], BF16)
        attn_es = ExitStack()
        sba = lambda name, shape, dt: attn_es.enter_context(nc.sbuf_tensor(name, shape, dt))
        tcos = sba("tcos", [128, S], F32)
        tsgn = sba("tsgn", [128, S], F32)
        PS = [es.enter_context(nc.psum_tensor(f"ps{i}", [128, 512], F32)) for i in range(8)]

        block = es.enter_context(nc.Block())
        x = Ctx(nc, block, [
            "dv", "pe", "ac", "p0", "g2", "sw", "qs", "as", "xs", "st", "dn",
            "fxa", "fxb", "lba", "lbb", "lqa", "lqb",
            "dqa", "dqb", "dsa", "dsb",
            "wla", "wlb", "p3a", "p3b", "xla", "w4a", "w4b", "wda", "wdb"])
        deq = Dequant(
            x,
            (dq_pk, dq_sc, dq_lo, dq_hi, dq_b8, dq_mg, dq_t1, dq_t2, dq_pl),
            (gpk, upk, dpk, gsc, usc, dsc),
            (wg_s, wu_s, wd_s),
        )

        CST = {"zero": 0, "one": 1, "neg_lninvf": 2, "half_pi": 3,
               "inv2pi": 4, "neg2pi": 5, "cap": 6, "tscale": 7, "eps": 8,
               "invh": 9, "neginvh": 10}

        def p0_g(g):
            x.I(g.iota(dq_lo[:, 0:1], pattern=[[0, 1]], channel_multiplier=1), "p0", 1)
            x.I(g.iota(dq_hi[:, 0:128], pattern=[[1, 128]], channel_multiplier=0), "p0", 1)
            x.I(g.iota(dq_pk[:, 0:S], pattern=[[1, S]], channel_multiplier=0), "p0", 1)
            g.wait_ge(x.sems["p0"], 3)
            x.I(g.tensor_copy(out=ip[:], in_=dq_lo[:, 0:1]), "p0", 1)
            g.wait_ge(x.sems["p0"], 4)
            x.I(g.tensor_scalar(out=iden[:], in0=dq_hi[:, 0:128], scalar1=ip[:],
                                scalar2=None, op0=ALU.is_equal), "p0", 1)
            x.I(g.tensor_scalar(out=idenf[:], in0=dq_hi[:, 0:128], scalar1=ip[:],
                                scalar2=None, op0=ALU.is_equal), "p0", 1)
            ins = g.tensor_scalar(out=mask[:], in0=dq_hi[:, 0:128], scalar1=ip[:],
                                  scalar2=-1.0e8, op0=ALU.is_lt, op1=ALU.mult)
            x.I(ins, "p0", 1)
        _body(block, "gpsimd", p0_g)

        def p0_v(v):
            for name, col in CST.items():
                val = {"zero": 0.0, "one": 1.0, "neg_lninvf": -LN1E4 / 64.0,
                       "half_pi": PI / 2.0, "inv2pi": INV2PI, "neg2pi": -TWO_PI,
                       "cap": CAP, "tscale": SCALE / CAP, "eps": EPS,
                       "invh": 1.0 / HID, "neginvh": -1.0 / HID}[name]
                v.memset(cst[:, col:col + 1], val)
            x.W(v, "vector", "p0", 7)
            ins = v.tensor_copy(out=invf[:], in_=ip[:])
            x.I(ins, "dv", 1)
        _body(block, "vector", p0_v)

        def p0_s(s):
            x.W(s, "scalar", "dv", 1)
            x.W(s, "scalar", "p0", 7)
            ins = s.activation(out=invf[0:64, :], in_=ip[0:64, :], func=AF.Exp,
                               bias=cst[0:64, CST["zero"]:CST["zero"] + 1],
                               scale=cst[0:64, CST["neg_lninvf"]:CST["neg_lninvf"] + 1])
            x.I(ins, "ac", 1)
        _body(block, "scalar", p0_s)

        def p0_g2(g):
            x.W(g, "gpsimd", "ac", 1)
            ins = g.dma_start(out=invf[64:128, :], in_=invf[0:64, :])
            x.I(ins, "g2", 16)
        _body(block, "gpsimd", p0_g2)

        dbg = sb("dbg", [128, 4 * S], F32) if "d_p0" in devt else None

        def p0_v2(v):
            x.W(v, "vector", "ac", 1)
            x.W(v, "vector", "g2")
            v.tensor_scalar(out=tcos[:], in0=dq_pk[:, 0:S], scalar1=invf[:], scalar2=None,
                            op0=ALU.mult)
            if dbg is not None:
                v.tensor_copy(out=dbg[:, 0:S], in_=tcos[:])
            tmpf = dq_sc[:, 0:S]
            tmpi = dq_pk[:, 0:S]

            def reduce_(ang):
                v.tensor_scalar(out=tmpf, in0=ang, scalar1=INV2PI, scalar2=None,
                                op0=ALU.mult)
                v.tensor_copy(out=tmpi, in_=tmpf)
                v.tensor_copy(out=tmpf, in_=tmpi)
                v.scalar_tensor_tensor(out=ang, in0=tmpf, scalar=-TWO_PI, in1=ang,
                                       op0=ALU.mult, op1=ALU.add)
                v.tensor_scalar(out=tmpf, in0=ang, scalar1=PI_LO, scalar2=None,
                                op0=ALU.is_gt)
                v.scalar_tensor_tensor(out=ang, in0=tmpf, scalar=-TWO_PI, in1=ang,
                                       op0=ALU.mult, op1=ALU.add)
                return v.tensor_scalar(out=ang, in0=ang, scalar1=PI_LO, scalar2=-PI_LO,
                                       op0=ALU.min, op1=ALU.max)

            v.tensor_copy(out=tsgn[:], in_=tcos[:])
            reduce_(tsgn[:])
            if dbg is not None:
                v.tensor_copy(out=dbg[:, S:2 * S], in_=tsgn[:])
            v.tensor_scalar(out=tcos[:], in0=tcos[:], scalar1=PI / 2.0, scalar2=None,
                            op0=ALU.add)
            ins = reduce_(tcos[:])
            if dbg is not None:
                ins = v.tensor_copy(out=dbg[:, 2 * S:3 * S], in_=tcos[:])
            x.I(ins, "dv", 1)
        _body(block, "vector", p0_v2)

        def p0_s2(s):
            x.W(s, "scalar", "dv")
            one = cst[:, CST["one"]:CST["one"] + 1]
            zero = cst[:, CST["zero"]:CST["zero"] + 1]
            s.activation(out=tsgn[:], in_=tsgn[:], func=AF.Sin, bias=zero, scale=one)
            ins = s.activation(out=tcos[:], in_=tcos[:], func=AF.Sin, bias=zero, scale=one)
            x.I(ins, "ac", 1)
        _body(block, "scalar", p0_s2)

        def p0_v3(v):
            x.W(v, "vector", "ac")
            ins = v.tensor_scalar(out=tsgn[0:64, :], in0=tsgn[0:64, :], scalar1=-1.0,
                                  scalar2=None, op0=ALU.mult)
            x.I(ins, "dv", 1)
        _body(block, "vector", p0_v3)
        trig_done = x.c["dv"]
        deq.start_gate = trig_done

        deq.emit(block, 4)

        if "d_trig" in devt:
            def dvo(g):
                x.W(g, "gpsimd", "dv", trig_done)
                x.I(g.dma_start(out=devt["d_trig"][0:128, :], in_=tcos[:]), "st", 16)
                x.I(g.dma_start(out=devt["d_trig"][128:256, :], in_=tsgn[:]), "st", 16)
                if dbg is not None:
                    x.I(g.dma_start(out=devt["d_p0"][:, :], in_=dbg[:]), "st", 16)
            _body(block, "gpsimd", dvo)

        if "d_wg" in devt:
            deq.finish(block)

            def dvw(g):
                x.W(g, "gpsimd", "dsa")
                x.W(g, "gpsimd", "dsb")
                x.I(g.dma_start(out=devt["d_wg"][:, :, :, :], in_=wg_s[:, :, :, :]), "st", 16)
                x.I(g.dma_start(out=devt["d_wu"][:, :, :, :], in_=wu_s[:, :, :, :]), "st", 16)
                x.I(g.dma_start(out=devt["d_wd"][:, :, :], in_=wd_s[:, :, :]), "st", 16)
                g.wait_ge(x.sems["st"], x.c["st"])
            _body(block, "gpsimd", dvw)
            return nc

        xlf = [sba("xlf0", [128, 1024], F32), sba("xlf1", [128, 1024], F32)]
        ubuf = sba("ubuf", [128, 16 * 1024], BF16)
        xfm = ubuf
        bandf = sba("bandf", [128, 2 * 2048], F32)
        bandb = sba("bandb", [128, 2 * 2048], BF16)
        qsw = sba("qsw", [128, 512], F32)
        qf = sba("qf", [128, 512], F32)
        rtmp = sba("rtmp", [128, 512], F32)
        qstg = sba("qstg", [128, 2 * 512], BF16)
        kfm = sba("kfm", [128, NKV * S], BF16)
        vsb = sba("vsb", [128, NKV * 16 * 128], BF16)
        vtmp = sba("vtmp", [128, 512], BF16)
        qh = sba("qh", [128, 2 * 2048], BF16)
        ptx = sba("ptx", [128, 2 * 512], BF16)
        ttmp = sba("ttmp", [128, 2 * 512], F32)
        drec = sba("drec", [128, 8], F32)
        drow4 = sba("drow4", [4, 2 * 128], F32)
        drep = sba("drep", [128, 2 * 512], F32)
        onesb = sba("onesb", [128, 1], BF16)
        aost = sba("aost", [128, 4 * 512], BF16)

        def p0_ones(v):
            x.I(v.memset(onesb[:], 1.0), "dv", 1)
        _body(block, "vector", p0_ones)

        bank_free = [("dv", 0)] * 8
        pbv = PS[7][:].bitcast(BF16)

        def use_bank(e, ename, i):
            sem, val = bank_free[i]
            x.W(e, ename, sem, val)

        def q_rope_store(b, tc, bank, sglob, s_):
            sl = tc % 2
            pq = PS[bank][:]

            def v_r1(v):
                x.W(v, "vector", "pe")
                x.W(v, "vector", "sw")
                v.tensor_tensor(out=rtmp[:], in0=pq, in1=tcos[:, sglob:sglob + 512], op=ALU.mult)
                ins = v.tensor_copy(out=qf[:], in_=pq)
                x.I(ins, "dv", 1)
            _body(block, "vector", v_r1)
            bank_free[bank] = ("dv", x.c["dv"])

            def g_swap(g):
                x.W(g, "gpsimd", "dv")
                x.I(g.dma_start(out=qsw[0:64, :], in_=qf[64:128, :]), "sw", 16)
                x.I(g.dma_start(out=qsw[64:128, :], in_=qf[0:64, :]), "sw", 16)
            _body(block, "gpsimd", g_swap)

            def v_rope(v):
                x.W(v, "vector", "sw")
                if b < NH:
                    x.W(v, "vector", "qs")
                v.tensor_tensor(out=qsw[:], in0=qsw[:], in1=tsgn[:, sglob:sglob + 512], op=ALU.mult)
                if b < NH:
                    dst = qstg[:, 512 * sl:512 * sl + 512]
                else:
                    dst = kfm[:, (b - NH) * S + sglob:(b - NH) * S + sglob + 512]
                ins = v.tensor_tensor(out=dst, in0=rtmp[:], in1=qsw[:], op=ALU.add)
                x.I(ins, "dv", 1)
            _body(block, "vector", v_rope)

            if b < NH:
                def g_store(g):
                    x.W(g, "gpsimd", "dv")
                    x.W(g, "gpsimd", "lqa")
                    x.W(g, "gpsimd", "lqb")
                    x.I(g.dma_start(out=q_s[b, :, sglob:sglob + 512],
                                    in_=qstg[:, 512 * sl:512 * sl + 512]), "qs", 16)
                _body(block, "gpsimd", g_store)

        def v_evac(b, tc, bank, sglob, s_):
            kv = b - NH - NKV
            def v_cp(v):
                x.W(v, "vector", "pe")
                ins = v.tensor_copy(out=vtmp[:], in_=PS[bank][:])
                x.I(ins, "dv", 1)
            _body(block, "vector", v_cp)
            bank_free[bank] = ("dv", x.c["dv"])
            pslot = (tc % 2) * 512

            def t_tr(t):
                x.W(t, "tensor", "dv")
                use_bank(t, "tensor", 7)
                for j in range(4):
                    ins = t.transpose(pbv[:, pslot + j * 128:pslot + j * 128 + 128],
                                      in_=vtmp[:, j * 128:j * 128 + 128], identity=iden[:])
                x.I(ins, "pe", 1)
            _body(block, "tensor", t_tr)

            kt0 = sglob // 128
            def v_ev(v):
                x.W(v, "vector", "pe")
                ins = v.tensor_copy(out=vsb[:, (kv * 16 + kt0) * 128:(kv * 16 + kt0) * 128 + 512],
                                    in_=pbv[:, pslot:pslot + 512])
                x.I(ins, "dv", 1)
            _body(block, "vector", v_ev)
            bank_free[7] = ("dv", x.c["dv"])

        def p1_seq(s_):
            for th in range(2):
                cast_done = [0, 0]
                for fb in range(16):
                    sl = fb % 2
                    fxs = "fxa" if sl == 0 else "fxb"

                    def s_xl(sy, s_=s_, th=th, fb=fb, sl=sl, fxs=fxs):
                        x.W(sy, "sync", "dv", cast_done[sl])
                        x.I(sy.dma_start(out=xlf[sl][:],
                                         in_=xt_in[s_, fb * 128:fb * 128 + 128,
                                                   th * 1024:th * 1024 + 1024]), fxs, 16)
                    _body(block, "sync", s_xl)

                    def v_cast(v, fb=fb, sl=sl, fxs=fxs):
                        x.W(v, "vector", fxs)
                        x.W(v, "vector", "pe")
                        x.W(v, "vector", "ac")
                        x.W(v, "vector", "st")
                        ins = v.tensor_copy(out=xfm[:, fb * 1024:fb * 1024 + 1024],
                                            in_=xlf[sl][:])
                        x.I(ins, "dv", 1)
                    _body(block, "vector", v_cast)
                    cast_done[sl] = x.c["dv"]
                xfm_done = x.c["dv"]
                if "d_xfm" in devt:
                    def dvxf(g, s_=s_, th=th, xfm_done=xfm_done):
                        x.W(g, "gpsimd", "dv", xfm_done)
                        x.I(g.dma_start(out=devt["d_xfm"][2 * s_ + th], in_=xfm[:, :]), "st", 16)
                    _body(block, "gpsimd", dvxf)

                band_cast_done = [0, 0]
                for b in range(NBANDS):
                    sl = b % 2
                    lbs = "lba" if sl == 0 else "lbb"

                    def s_ld(sy, b=b, sl=sl):
                        x.W(sy, "sync", "pe", band_mm_done.get(sl, 0))
                        x.I(sy.dma_start(out=bandf[:, 2048 * sl:2048 * sl + 2048],
                                         in_=wqb[b].rearrange("hb p m -> p hb m")), lbs, 16)
                    _body(block, "sync", s_ld)

                    def v_cast(v, sl=sl, lbs=lbs):
                        x.W(v, "vector", lbs)
                        x.W(v, "vector", "pe", band_mm_done.get(sl, 0))
                        ins = v.tensor_copy(out=bandb[:, 2048 * sl:2048 * sl + 2048],
                                            in_=bandf[:, 2048 * sl:2048 * sl + 2048])
                        x.I(ins, "dv", 1)
                    _body(block, "vector", v_cast)
                    band_cast_done[sl] = x.c["dv"]

                    for tc in range(2):
                        bank = (b * 2 + tc) % 4
                        sglob = th * 1024 + tc * 512

                        def t_mm(t, sl=sl, bank=bank, tc=tc):
                            x.W(t, "tensor", "dv", max(band_cast_done[sl], xfm_done))
                            use_bank(t, "tensor", bank)
                            for hb in range(16):
                                ins = t.matmul(PS[bank][:],
                                               lhsT=bandb[:, 2048 * sl + hb * 128:2048 * sl + hb * 128 + 128],
                                               rhs=xfm[:, hb * 1024 + tc * 512:hb * 1024 + tc * 512 + 512],
                                               start=(hb == 0), stop=(hb == 15))
                            x.I(ins, "pe", 1)
                        _body(block, "tensor", t_mm)

                        if b < NH + NKV:
                            q_rope_store(b, tc, bank, sglob, s_)
                        else:
                            v_evac(b, tc, bank, sglob, s_)
                    band_mm_done[sl] = x.c["pe"]
                deq.emit(block, 2)

        def p2_seq(s_):
            qh_used = [0, 0]
            kv_ready = x.c["dv"]
            st2 = {"tanh_done": [0, 0], "pv_done": [0, 0], "evac_dv": 0,
                   "recip_dv": 0}
            for hd in range(NH):
                kv = hd // 8
                sl = hd % 2
                lqs = "lqa" if sl == 0 else "lqb"

                def s_qld(sy, hd=hd, sl=sl, lqs=lqs):
                    x.W(sy, "sync", "qs")
                    x.W(sy, "sync", "pe", qh_used[sl])
                    x.I(sy.dma_start(out=qh[:, 2048 * sl:2048 * sl + 2048], in_=q_s[hd]), lqs, 16)
                _body(block, "sync", s_qld)
                qld_here = x.c[lqs]

                for g_ in range(4):
                    nkt = 4 * g_ + 4
                    dsl = g_ % 2
                    for kt in range(nkt):
                        qkb = kt % 2
                        psl = kt % 2
                        diag = kt >= 4 * g_
                        off = max(0, kt - 4 * g_) * 128

                        def t_qk(t, sl=sl, g_=g_, kt=kt, qkb=qkb, kv=kv,
                                 qld_here=qld_here, lqs=lqs):
                            x.W(t, "tensor", lqs, qld_here)
                            x.W(t, "tensor", "dv", kv_ready)
                            x.W(t, "tensor", "ac", st2["tanh_done"][qkb])
                            ins = t.matmul(PS[qkb][:],
                                           lhsT=kfm[:, kv * S + kt * 128:kv * S + kt * 128 + 128],
                                           rhs=qh[:, 2048 * sl + g_ * 512:2048 * sl + g_ * 512 + 512],
                                           start=True, stop=True)
                            x.I(ins, "pe", 1)
                        _body(block, "tensor", t_qk)
                        qk_pe = x.c["pe"]

                        msk_dv = 0
                        if diag:
                            j = kt - 4 * g_

                            def v_msk(v, qkb=qkb, j=j, qk_pe=qk_pe):
                                x.W(v, "vector", "pe", qk_pe)
                                ins = v.tensor_tensor(
                                    out=PS[qkb][:, j * 128:j * 128 + 128],
                                    in0=PS[qkb][:, j * 128:j * 128 + 128],
                                    in1=mask[:], op=ALU.add)
                                x.I(ins, "dv", 1)
                            _body(block, "vector", v_msk)
                            msk_dv = x.c["dv"]

                        def s_t(sc, qkb=qkb, psl=psl, qk_pe=qk_pe, msk_dv=msk_dv):
                            x.W(sc, "scalar", "pe", qk_pe)
                            x.W(sc, "scalar", "dv", msk_dv)
                            ins = sc.activation(out=ttmp[:, 512 * psl:512 * psl + 512],
                                                in_=PS[qkb][:], func=AF.Tanh,
                                                bias=cst[:, CST["zero"]:CST["zero"] + 1],
                                                scale=cst[:, CST["tscale"]:CST["tscale"] + 1])
                            x.I(ins, "ac", 1)
                        _body(block, "scalar", s_t)
                        st2["tanh_done"][qkb] = x.c["ac"]

                        def s_e(sc, psl=psl):
                            x.W(sc, "scalar", "pe", st2["pv_done"][psl])
                            ins = sc.activation(out=ptx[:, 512 * psl:512 * psl + 512],
                                                in_=ttmp[:, 512 * psl:512 * psl + 512],
                                                func=AF.Exp,
                                                bias=cst[:, CST["zero"]:CST["zero"] + 1],
                                                scale=cst[:, CST["cap"]:CST["cap"] + 1])
                            x.I(ins, "ac", 1)
                        _body(block, "scalar", s_e)
                        exp_ac = x.c["ac"]

                        def t_pv(t, g_=g_, kt=kt, nkt=nkt, psl=psl, off=off, kv=kv,
                                 exp_ac=exp_ac):
                            x.W(t, "tensor", "ac", exp_ac)
                            if kt == 0:
                                x.W(t, "tensor", "dv",
                                    max(st2["evac_dv"], st2["recip_dv"]))
                            t.matmul(PS[4][:, off:512] if off else PS[4][:],
                                     lhsT=vsb[:, (kv * 16 + kt) * 128:(kv * 16 + kt) * 128 + 128],
                                     rhs=ptx[:, 512 * psl + off:512 * psl + 512],
                                     start=(kt == 0), stop=(kt == nkt - 1))
                            for j in range(max(0, kt - 4 * g_), 4):
                                ins = t.matmul(PS[5][:, j:j + 1],
                                               lhsT=ptx[:, 512 * psl + j * 128:512 * psl + j * 128 + 128],
                                               rhs=onesb[:],
                                               start=(kt == 0 and j == 0),
                                               stop=(kt == nkt - 1 and j == 3))
                            x.I(ins, "pe", 1)
                        _body(block, "tensor", t_pv)
                        st2["pv_done"][psl] = x.c["pe"]
                    pv_all = x.c["pe"]

                    def v_rc(v, dsl=dsl, pv_all=pv_all):
                        x.W(v, "vector", "pe", pv_all)
                        x.W(v, "vector", "pe", st2.get(("tr6", dsl), 0))
                        ins = v.reciprocal(out=drec[:, 4 * dsl:4 * dsl + 4], in_=PS[5][:, 0:4])
                        x.I(ins, "dv", 1)
                    _body(block, "vector", v_rc)
                    st2["recip_dv"] = x.c["dv"]

                    def t_dntr(t, dsl=dsl):
                        x.W(t, "tensor", "dv", x.c["dv"])
                        x.W(t, "tensor", "dv", st2.get("cp6", 0))
                        ins = t.transpose(PS[6][0:4, 0:128],
                                          in_=drec[:, 4 * dsl:4 * dsl + 4], identity=idenf[:])
                        x.I(ins, "pe", 1)
                    _body(block, "tensor", t_dntr)
                    st2[("tr6", dsl)] = x.c["pe"]

                    def v_cp6(v, dsl=dsl):
                        x.W(v, "vector", "pe")
                        x.W(v, "vector", "dn")
                        ins = v.tensor_copy(out=drow4[:, 128 * dsl:128 * dsl + 128],
                                            in_=PS[6][0:4, 0:128])
                        x.I(ins, "dv", 1)
                    _body(block, "vector", v_cp6)
                    st2["cp6"] = x.c["dv"]

                    def g_dn(g, dsl=dsl):
                        x.W(g, "gpsimd", "dv")
                        x.I(g.dma_start(out=dn_s[dsl],
                                        in_=drow4[:, 128 * dsl:128 * dsl + 128]), "dn", 16)
                        x.W(g, "gpsimd", "dn")
                        x.I(g.dma_start(
                            out=drep[:, 512 * dsl:512 * dsl + 512]
                            .rearrange("p (j i) -> p j i", j=4),
                            in_=dn_s[dsl].unsqueeze(0).broadcast_to([128, 4, 128])), "dn", 16)
                    _body(block, "gpsimd", g_dn)

                    asl = g_ % 4

                    def v_ev2(v, asl=asl, dsl=dsl):
                        x.W(v, "vector", "pe")
                        x.W(v, "vector", "dn")
                        x.W(v, "vector", "as")
                        ins = v.tensor_tensor(out=aost[:, asl * 512:asl * 512 + 512],
                                              in0=PS[4][:],
                                              in1=drep[:, 512 * dsl:512 * dsl + 512],
                                              op=ALU.mult)
                        x.I(ins, "dv", 1)
                    _body(block, "vector", v_ev2)
                    st2["evac_dv"] = x.c["dv"]

                    def g_ao(g, hd=hd, g_=g_, asl=asl, s_=s_):
                        x.W(g, "gpsimd", "dv")
                        x.I(g.dma_start(out=ao_s[s_, hd, :, g_ * 512:g_ * 512 + 512],
                                        in_=aost[:, asl * 512:asl * 512 + 512]), "as", 16)
                    _body(block, "gpsimd", g_ao)
                    deq.emit(block, 2)
                qh_used[sl] = x.c["pe"]

        band_mm_done = {}
        for s_ in range(SEQ):
            band_mm_done.clear()
            p1_seq(s_)
            p2_seq(s_)

        if "d_q" in devt:
            def dvq(g):
                x.W(g, "gpsimd", "qs")
                x.W(g, "gpsimd", "as")
                x.W(g, "gpsimd", "dv")
                x.I(g.dma_start(out=devt["d_q"][:, :, :], in_=q_s[:, :, :]), "st", 16)
                x.I(g.dma_start(out=devt["d_k"][0:128, :], in_=kfm[:]), "st", 16)
                x.I(g.dma_start(out=devt["d_v"][0:128, :], in_=vsb[:]), "st", 16)
                x.I(g.dma_start(out=devt["d_ao"][:, :, :, :], in_=ao_s[:, :, :, :]), "st", 16)
                g.wait_ge(x.sems["st"], x.c["st"])
            _body(block, "gpsimd", dvq)

        attn_es.close()
        es3 = ExitStack()
        sb3 = lambda name, shape, dt: es3.enter_context(nc.sbuf_tensor(name, shape, dt))
        wo_sb = sb3("wo_sb", [128, 16 * 2048], BF16)
        wof = sb3("wof", [128, 2 * 2048], F32)
        ao_t = sb3("ao_t", [128, 2 * 2048], BF16)
        xres = sb3("xres", [128, 2 * 2048], F32)
        resb = sb3("resb", [128, 2048], F32)
        sqd = sb3("sqd", [128, 2048], F32)
        xnst = sb3("xnst", [128, 2 * 2048], BF16)
        lnsc = sb3("lnsc", [128, 16], F32)
        vg3 = sb3("vg3", [128, 64], F32)
        FENCE3 = ("pe", "dv", "ac", "st", "qs", "sw", "as")
        fence = {k: x.c[k] for k in FENCE3}

        def p3_seq(s_, first):
            cast_done = [0, 0]
            for fb in range(16):
                sl = fb % 2
                wls = "wla" if sl == 0 else "wlb"

                def s_ld(sy, fb=fb, sl=sl, wls=wls, first=first):
                    if first and fb < 2:
                        for k_ in FENCE3:
                            x.W(sy, "sync", k_, fence[k_])
                    x.W(sy, "sync", "dv", cast_done[sl])
                    x.I(sy.dma_start(out=wof[:, 2048 * sl:2048 * sl + 2048],
                                     in_=wob[fb]), wls, 16)
                _body(block, "sync", s_ld)

                def v_cast(v, fb=fb, sl=sl, wls=wls):
                    x.W(v, "vector", wls)
                    ins = v.tensor_copy(out=wo_sb[:, fb * 2048:fb * 2048 + 2048],
                                        in_=wof[:, 2048 * sl:2048 * sl + 2048])
                    x.I(ins, "dv", 1)
                _body(block, "vector", v_cast)
                cast_done[sl] = x.c["dv"]
            wo_done = x.c["dv"]

            ao_used = [0, 0]
            xres_used = [0, 0]
            resb_free = [0]
            xnst_used = [0, 0]
            for t128 in range(16):
                sl = t128 % 2
                bg = (t128 % 2) * 4
                p3s = "p3a" if sl == 0 else "p3b"

                def g_ld(g, s_=s_, t128=t128, sl=sl, p3s=p3s, first=first):
                    if first and t128 < 2:
                        for k_ in FENCE3:
                            x.W(g, "gpsimd", k_, fence[k_])
                    x.W(g, "gpsimd", "pe", ao_used[sl])
                    x.W(g, "gpsimd", "dv", xres_used[sl])
                    x.W(g, "gpsimd", "as")
                    src = ao_s[s_, :, :, t128 * 128:t128 * 128 + 128].rearrange("f p t -> p f t")
                    x.I(g.dma_start(out=ao_t[:, 2048 * sl:2048 * sl + 2048]
                                    .rearrange("p (f t) -> p f t", f=16), in_=src), p3s, 16)
                    x.I(g.dma_start(out=xres[:, 2048 * sl:2048 * sl + 2048],
                                    in_=x_in[s_, t128 * 128:t128 * 128 + 128, :]), p3s, 16)
                _body(block, "gpsimd", g_ld)
                p3_here = x.c[p3s]

                def t_mm(t, sl=sl, bg=bg, p3s=p3s, p3_here=p3_here, wo_done=wo_done):
                    x.W(t, "tensor", p3s, p3_here)
                    x.W(t, "tensor", "dv", wo_done)
                    for hc in range(4):
                        use_bank(t, "tensor", bg + hc)
                    for ft in range(16):
                        for hc in range(4):
                            ins = t.matmul(PS[bg + hc][:],
                                           lhsT=ao_t[:, 2048 * sl + ft * 128:2048 * sl + ft * 128 + 128],
                                           rhs=wo_sb[:, ft * 2048 + hc * 512:ft * 2048 + hc * 512 + 512],
                                           start=(ft == 0), stop=(ft == 15))
                    x.I(ins, "pe", 1)
                _body(block, "tensor", t_mm)
                ao_used[sl] = x.c["pe"]
                mm_here = x.c["pe"]

                def v_res(v, sl=sl, bg=bg, mm_here=mm_here):
                    x.W(v, "vector", "pe", mm_here)
                    x.W(v, "vector", "ac", resb_free[0])
                    for hc in range(4):
                        ins = v.scalar_tensor_tensor(
                            out=resb[:, hc * 512:hc * 512 + 512], in0=PS[bg + hc][:],
                            scalar=1.0, in1=xres[:, 2048 * sl + hc * 512:2048 * sl + hc * 512 + 512],
                            op0=ALU.mult, op1=ALU.add, accum_out=lnsc[:, hc:hc + 1])
                    x.I(ins, "dv", 1)
                _body(block, "vector", v_res)
                xres_used[sl] = x.c["dv"]
                for hc in range(4):
                    bank_free[bg + hc] = ("dv", x.c["dv"])
                res_here = x.c["dv"]

                def s_sq(s, res_here=res_here):
                    x.W(s, "scalar", "dv", res_here)
                    for hc in range(4):
                        ins = s.activation(out=sqd[:, hc * 512:hc * 512 + 512],
                                           in_=resb[:, hc * 512:hc * 512 + 512], func=AF.Square,
                                           bias=cst[:, CST["zero"]:CST["zero"] + 1],
                                           scale=cst[:, CST["one"]:CST["one"] + 1],
                                           accum_out=lnsc[:, 4 + hc:5 + hc])
                    x.I(ins, "ac", 1)
                _body(block, "scalar", s_sq)
                sq_here = x.c["ac"]
                resb_free[0] = sq_here

                def v_ln(v, sq_here=sq_here):
                    x.W(v, "vector", "ac", sq_here)
                    v.reduce_sum(out=lnsc[:, 8:9], in_=lnsc[:, 0:4], axis=mybir.AxisListType.X)
                    v.reduce_sum(out=lnsc[:, 9:10], in_=lnsc[:, 4:8], axis=mybir.AxisListType.X)
                    v.tensor_scalar(out=lnsc[:, 10:11], in0=lnsc[:, 8:9], scalar1=1.0 / HID,
                                    scalar2=None, op0=ALU.mult)
                    v.tensor_scalar(out=lnsc[:, 11:12], in0=lnsc[:, 9:10], scalar1=1.0 / HID,
                                    scalar2=None, op0=ALU.mult)
                    v.tensor_scalar(out=lnsc[:, 12:13], in0=lnsc[:, 10:11], scalar1=-1.0,
                                    scalar2=None, op0=ALU.mult)
                    v.tensor_tensor(out=lnsc[:, 13:14], in0=lnsc[:, 10:11], in1=lnsc[:, 10:11],
                                    op=ALU.mult)
                    v.memset(vg3[:, 0:64], 0.0)
                    v.memset(vg3[:, 0:64], 1.0)
                    ins = v.tensor_tensor(out=lnsc[:, 14:15], in0=lnsc[:, 11:12],
                                          in1=lnsc[:, 13:14], op=ALU.subtract)
                    x.I(ins, "dv", 1)
                _body(block, "vector", v_ln)
                ln_here = x.c["dv"]

                def s_rstd(s, ln_here=ln_here):
                    x.W(s, "scalar", "dv", ln_here)
                    ins = s.activation(out=lnsc[:, 15:16], in_=lnsc[:, 14:15], func=AF.Sqrt,
                                       bias=cst[:, CST["eps"]:CST["eps"] + 1],
                                       scale=cst[:, CST["one"]:CST["one"] + 1])
                    x.I(ins, "ac", 1)
                _body(block, "scalar", s_rstd)
                rstd_here = x.c["ac"]

                def v_xn(v, sl=sl, rstd_here=rstd_here):
                    x.W(v, "vector", "ac", rstd_here)
                    x.W(v, "vector", "xs")
                    v.reciprocal(out=lnsc[:, 7:8], in_=lnsc[:, 15:16])
                    v.memset(vg3[:, 0:64], 0.0)
                    v.memset(vg3[:, 0:64], 1.0)
                    ins = v.tensor_scalar(out=xnst[:, 2048 * sl:2048 * sl + 2048], in0=resb[:],
                                          scalar1=lnsc[:, 12:13], scalar2=lnsc[:, 7:8],
                                          op0=ALU.add, op1=ALU.mult)
                    x.I(ins, "dv", 1)
                _body(block, "vector", v_xn)

                def g_st(g, s_=s_, t128=t128, sl=sl):
                    x.W(g, "gpsimd", "dv")
                    x.I(g.dma_start(out=xn_s[s_ * S + t128 * 128:s_ * S + t128 * 128 + 128, :],
                                    in_=xnst[:, 2048 * sl:2048 * sl + 2048]), "xs", 16)
                _body(block, "gpsimd", g_st)
                xnst_used[sl] = x.c["xs"]
                if t128 % 4 == 3:
                    deq.emit(block, 2)

        p3_seq(0, True)
        p3_seq(1, False)

        if "d_xn" in devt:
            def dvx(g):
                x.W(g, "gpsimd", "xs")
                x.I(g.dma_start(out=devt["d_xn"][:, :], in_=xn_s[:, :]), "st", 16)
            _body(block, "gpsimd", dvx)
        if devt and "d_wg" not in devt and "stop_p3" in devt:
            es3.close()
            deq_es.close()
            return nc

        deq.finish(block)
        es3.close()
        deq_es.close()

        es4 = ExitStack()
        sb4 = lambda name, shape, dt: es4.enter_context(nc.sbuf_tensor(name, shape, dt))
        acc = sb4("acc", [128, 8 * 2048], F32)
        xfm4 = sb4("xfm4", [128, 16 * TB], BF16)
        inter = sb4("inter", [128, 16 * TB], BF16)
        wgc = sb4("wgc", [128, 2 * 2048], BF16)
        wuc = sb4("wuc", [128, 2 * 2048], BF16)
        wds = sb4("wds", [128, 2 * 8192], BF16)
        xnt4 = sb4("xnt4", [128, 2048], BF16)
        gl4 = sb4("gl4", [128, 512], F32)
        res4 = sb4("res4", [128, 2048], F32)
        sq4 = sb4("sq4", [128, 2048], F32)
        xo4 = sb4("xo4", [128, 2048], F32)
        ln4 = sb4("ln4", [128, 16], F32)
        vg4 = sb4("vg4", [128, 64], F32)
        FENCE4 = ("pe", "dv", "ac", "st", "xs", "dsa", "dsb", "qs", "sw", "as")
        fence4 = {k: x.c[k] for k in FENCE4}

        s1_done_prev = [0]
        s2_done_prev = [0]
        xfm4_free = [0]
        state4 = {"xnt_used": 0, "xnt_used_dv": 0, "wg_used": [0, 0], "res_free": 0,
                  "res_free2": 0, "xo_used": 0, "wds_used": [0, 0]}

        for tb in range(4):
            for tt_ in range(8):
                gt0 = tb * TB + tt_ * 128

                def g_xl(g, gt0=gt0, tb=tb, tt_=tt_):
                    if tb == 0 and tt_ == 0:
                        for k_ in FENCE4:
                            x.W(g, "gpsimd", k_, fence4[k_])
                    x.W(g, "gpsimd", "pe", state4["xnt_used"])
                    x.W(g, "gpsimd", "dv", state4["xnt_used_dv"])
                    x.W(g, "gpsimd", "xs")
                    x.I(g.dma_start(out=xnt4[:], in_=xn_s[gt0:gt0 + 128, :]), "xla", 16)
                _body(block, "gpsimd", g_xl)
                g2h = x.c["xla"]

                for grp in range(4):
                    pslot = (grp % 2) * 512

                    def t_tr(t, grp=grp, pslot=pslot, g2h=g2h, tb=tb, tt_=tt_):
                        x.W(t, "tensor", "xla", g2h)
                        x.W(t, "tensor", "pe", s1_done_prev[0])
                        use_bank(t, "tensor", 7)
                        for j in range(4):
                            fb = grp * 4 + j
                            k_ = fb // 2
                            e_ = fb % 2
                            ins = t.transpose(pbv[:, pslot + j * 128:pslot + j * 128 + 128],
                                              in_=xnt4[:, 256 * k_ + e_:256 * k_ + 256:2],
                                              identity=iden[:])
                        x.I(ins, "pe", 1)
                    _body(block, "tensor", t_tr)

                    def v_ev(v, grp=grp, pslot=pslot, tt_=tt_):
                        x.W(v, "vector", "pe")
                        dst = xfm4[:, :].rearrange("p (f t) -> p f t", f=16)[
                            :, grp * 4:grp * 4 + 4, tt_ * 128:tt_ * 128 + 128]
                        ins = v.tensor_copy(out=dst, in_=pbv[:, pslot:pslot + 512]
                                            .rearrange("p (f t) -> p f t", f=4))
                        x.I(ins, "dv", 1)
                    _body(block, "vector", v_ev)
                    bank_free[7] = ("dv", x.c["dv"])
                state4["xnt_used"] = x.c["pe"]
            xfm4_done = x.c["dv"]

            def v_z(v):
                ins = v.memset(acc[:], 0.0)
                x.I(ins, "dv", 1)
            _body(block, "vector", v_z)

            for sec in range(NSEC):
                for ci in range(SECC):
                    c = sec * SECC + ci
                    slw = ci % 2

                    w4s = "w4a" if slw == 0 else "w4b"

                    def s_wl(sy, c=c, slw=slw, w4s=w4s, tb=tb, sec=sec, ci=ci):
                        if tb == 0 and sec == 0 and ci < 2:
                            for k_ in FENCE4:
                                x.W(sy, "sync", k_, fence4[k_])
                        x.W(sy, "sync", "dsa")
                        x.W(sy, "sync", "dsb")
                        x.W(sy, "sync", "pe", state4["wg_used"][slw])
                        x.I(sy.dma_start(out=wgc[:, 2048 * slw:2048 * slw + 2048],
                                         in_=wg_s[c].rearrange("p h n -> p (h n)")), w4s, 16)
                        x.I(sy.dma_start(out=wuc[:, 2048 * slw:2048 * slw + 2048],
                                         in_=wu_s[c].rearrange("p h n -> p (h n)")), w4s, 16)
                    _body(block, "sync", s_wl)
                    ldh = x.c[w4s]

                    bg = (ci % 2) * 2

                    for half in range(2):
                        def t_h(t, ci=ci, slw=slw, bg=bg, half=half, w4s=w4s, ldh=ldh,
                                xfm4_done=xfm4_done):
                            x.W(t, "tensor", w4s, ldh)
                            x.W(t, "tensor", "dv", xfm4_done)
                            use_bank(t, "tensor", bg)
                            use_bank(t, "tensor", bg + 1)
                            for hb in range(16):
                                t.matmul(PS[bg][:],
                                         lhsT=wgc[:, 2048 * slw + hb * 128:2048 * slw + hb * 128 + 128],
                                         rhs=xfm4[:, hb * TB + half * 512:hb * TB + half * 512 + 512],
                                         start=(hb == 0), stop=(hb == 15))
                            for hb in range(16):
                                ins = t.matmul(PS[bg + 1][:],
                                               lhsT=wuc[:, 2048 * slw + hb * 128:2048 * slw + hb * 128 + 128],
                                               rhs=xfm4[:, hb * TB + half * 512:hb * TB + half * 512 + 512],
                                               start=(hb == 0), stop=(hb == 15))
                            x.I(ins, "pe", 1)
                        _body(block, "tensor", t_h)
                        peh = x.c["pe"]

                        def s_gel(sc, bg=bg, peh=peh):
                            x.W(sc, "scalar", "pe", peh)
                            x.W(sc, "scalar", "dv", state4["res_free"])
                            ins2 = sc.activation(out=gl4[:], in_=PS[bg][:],
                                                 func=AF.Gelu_apprx_tanh,
                                                 bias=cst[:, CST["zero"]:CST["zero"] + 1],
                                                 scale=cst[:, CST["one"]:CST["one"] + 1])
                            x.I(ins2, "ac", 1)
                        _body(block, "scalar", s_gel)
                        bank_free[bg] = ("ac", x.c["ac"])

                        def v_int(v, ci=ci, bg=bg, half=half):
                            x.W(v, "vector", "ac")
                            x.W(v, "vector", "pe", s2_done_prev[0])
                            ins2 = v.tensor_tensor(
                                out=inter[:, ci * TB + half * 512:ci * TB + half * 512 + 512],
                                in0=gl4[:], in1=PS[bg + 1][:], op=ALU.mult)
                            x.I(ins2, "dv", 1)
                        _body(block, "vector", v_int)
                        bank_free[bg + 1] = ("dv", x.c["dv"])
                        state4["res_free"] = x.c["dv"]
                    state4["wg_used"][slw] = x.c["pe"]
                s1_done = x.c["dv"]
                if tb == 0 and sec == 0 and "d_int" in devt:
                    def dvi(g):
                        x.W(g, "gpsimd", "dv", s1_done)
                        x.I(g.dma_start(out=devt["d_int"][:, :], in_=inter[:]), "st", 16)
                    _body(block, "gpsimd", dvi)

                for hq in range(4):
                    slw = hq % 2

                    wdsem = "wda" if slw == 0 else "wdb"

                    def s_wd(sy, sec=sec, hq=hq, slw=slw, wdsem=wdsem, tb=tb):
                        if tb == 0 and sec == 0 and hq < 2:
                            for k_ in FENCE4:
                                x.W(sy, "sync", k_, fence4[k_])
                        x.W(sy, "sync", "dsa")
                        x.W(sy, "sync", "dsb")
                        x.W(sy, "sync", "pe", state4["wds_used"][slw])
                        src = wd_s[sec * SECC:sec * SECC + SECC, :, hq * 512:hq * 512 + 512]
                        x.I(sy.dma_start(out=wds[:, 8192 * slw:8192 * slw + 8192]
                                         .rearrange("p (c h) -> p c h", c=16),
                                         in_=src.rearrange("c p h -> p c h")), wdsem, 16)
                    _body(block, "sync", s_wd)
                    ldh2 = x.c[wdsem]

                    for t128 in range(8):
                        def t_s2(t, hq=hq, slw=slw, t128=t128, wdsem=wdsem, ldh2=ldh2,
                                 s1_done=s1_done):
                            x.W(t, "tensor", wdsem, ldh2)
                            x.W(t, "tensor", "dv", s1_done)
                            use_bank(t, "tensor", t128)
                            for ci in range(SECC):
                                ins = t.matmul(PS[t128][:],
                                               lhsT=inter[:, ci * TB + t128 * 128:ci * TB + t128 * 128 + 128],
                                               rhs=wds[:, 8192 * slw + ci * 512:8192 * slw + ci * 512 + 512],
                                               start=(ci == 0), stop=(ci == SECC - 1))
                            x.I(ins, "pe", 1)
                        _body(block, "tensor", t_s2)

                        def v_acc(v, hq=hq, t128=t128):
                            x.W(v, "vector", "pe")
                            dst = acc[:, t128 * 2048 + hq * 512:t128 * 2048 + hq * 512 + 512]
                            ins = v.tensor_tensor(out=dst, in0=dst, in1=PS[t128][:], op=ALU.add)
                            x.I(ins, "dv", 1)
                        _body(block, "vector", v_acc)
                        bank_free[t128] = ("dv", x.c["dv"])
                    state4["wds_used"][slw] = x.c["pe"]
                s2_done_prev[0] = x.c["pe"]
            acc_done = x.c["dv"]
            if tb == 0 and "d_acc" in devt:
                def dva(g):
                    x.W(g, "gpsimd", "dv", acc_done)
                    x.I(g.dma_start(out=devt["d_acc"][:, :], in_=acc[:]), "st", 16)
                _body(block, "gpsimd", dva)

            for tt_ in range(8):
                gt0 = tb * TB + tt_ * 128
                s_ = gt0 // S
                row = gt0 % S

                def g_xl2(g, gt0=gt0):
                    x.W(g, "gpsimd", "pe", state4["xnt_used"])
                    x.W(g, "gpsimd", "dv", state4["xnt_used_dv"])
                    x.I(g.dma_start(out=xnt4[:], in_=xn_s[gt0:gt0 + 128, :]), "xla", 16)
                _body(block, "gpsimd", g_xl2)
                g2h2 = x.c["xla"]

                def v_r2(v, g2h2=g2h2, acc_done=acc_done, tt_=tt_):
                    x.W(v, "vector", "xla", g2h2)
                    x.W(v, "vector", "dv", acc_done)
                    x.W(v, "vector", "ac", state4["res_free2"])
                    ins = v.scalar_tensor_tensor(out=res4[:], in0=acc[:, tt_ * 2048:tt_ * 2048 + 2048],
                                                 scalar=1.0, in1=xnt4[:],
                                                 op0=ALU.mult, op1=ALU.add,
                                                 accum_out=ln4[:, 0:1])
                    x.I(ins, "dv", 1)
                _body(block, "vector", v_r2)
                state4["xnt_used_dv"] = x.c["dv"]
                rh = x.c["dv"]

                def s_sq2(sc, rh=rh):
                    x.W(sc, "scalar", "dv", rh)
                    ins = sc.activation(out=sq4[:], in_=res4[:], func=AF.Square,
                                        bias=cst[:, CST["zero"]:CST["zero"] + 1],
                                        scale=cst[:, CST["one"]:CST["one"] + 1],
                                        accum_out=ln4[:, 1:2])
                    x.I(ins, "ac", 1)
                _body(block, "scalar", s_sq2)
                state4["res_free2"] = x.c["ac"]
                sqh = x.c["ac"]

                def v_ln2(v, sqh=sqh):
                    x.W(v, "vector", "ac", sqh)
                    v.tensor_scalar(out=ln4[:, 2:3], in0=ln4[:, 0:1], scalar1=1.0 / HID,
                                    scalar2=None, op0=ALU.mult)
                    v.tensor_scalar(out=ln4[:, 3:4], in0=ln4[:, 1:2], scalar1=1.0 / HID,
                                    scalar2=None, op0=ALU.mult)
                    v.tensor_scalar(out=ln4[:, 4:5], in0=ln4[:, 2:3], scalar1=-1.0,
                                    scalar2=None, op0=ALU.mult)
                    v.tensor_tensor(out=ln4[:, 5:6], in0=ln4[:, 2:3], in1=ln4[:, 2:3], op=ALU.mult)
                    v.memset(vg4[:, 0:64], 0.0)
                    v.memset(vg4[:, 0:64], 1.0)
                    ins = v.tensor_tensor(out=ln4[:, 6:7], in0=ln4[:, 3:4], in1=ln4[:, 5:6],
                                          op=ALU.subtract)
                    x.I(ins, "dv", 1)
                _body(block, "vector", v_ln2)
                lh = x.c["dv"]

                def s_sd(sc, lh=lh):
                    x.W(sc, "scalar", "dv", lh)
                    ins = sc.activation(out=ln4[:, 7:8], in_=ln4[:, 6:7], func=AF.Sqrt,
                                        bias=cst[:, CST["eps"]:CST["eps"] + 1],
                                        scale=cst[:, CST["one"]:CST["one"] + 1])
                    x.I(ins, "ac", 1)
                _body(block, "scalar", s_sd)
                sdh = x.c["ac"]

                def v_out(v, sdh=sdh):
                    x.W(v, "vector", "ac", sdh)
                    x.W(v, "vector", "st")
                    v.reciprocal(out=ln4[:, 8:9], in_=ln4[:, 7:8])
                    v.memset(vg4[:, 0:64], 0.0)
                    v.memset(vg4[:, 0:64], 1.0)
                    ins = v.tensor_scalar(out=xo4[:], in0=res4[:],
                                          scalar1=ln4[:, 4:5], scalar2=ln4[:, 8:9],
                                          op0=ALU.add, op1=ALU.mult)
                    x.I(ins, "dv", 1)
                _body(block, "vector", v_out)
                oh = x.c["dv"]

                def g_out(g, s_=s_, row=row, oh=oh):
                    x.W(g, "gpsimd", "dv", oh)
                    x.I(g.dma_start(out=out[s_, row:row + 128, :], in_=xo4[:]), "st", 16)
                _body(block, "gpsimd", g_out)
                state4["xo_used"] = x.c["st"]

        es4.close()

        def fin(g):
            g.wait_ge(x.sems["st"], x.c["st"])
        _body(block, "gpsimd", fin)

    return nc


DEV_SHAPES = {
    "d_xfm": ([4, 128, 16 * 1024], BF16),
    "d_trig": ([256, S], F32),
    "d_p0": ([128, 4 * S], F32),
    "d_q": ([NH, 128, S], BF16),
    "d_k": ([128, NKV * S], BF16),
    "d_v": ([128, NKV * 16 * 128], BF16),
    "d_ao": ([SEQ, NH, 128, S], BF16),
    "d_xn": ([SEQ * S, HID], BF16),
    "d_int": ([128, 16 * TB], BF16),
    "d_acc": ([128, 8 * 2048], F32),
    "d_wg": ([NCH, 128, 16, 128], BF16),
    "d_wu": ([NCH, 128, 16, 128], BF16),
    "d_wd": ([NCH, 128, HID], BF16),
}


def _perm16384():
    return np.arange(MLP).reshape(64, 128, 2).transpose(0, 2, 1).reshape(-1)


def prep_inputs(L0_attn_hidden_states, L0_attn_Wqkv, L0_attn_Wo,
                L0_ffn_gate_W_packed, L0_ffn_gate_scales,
                L0_ffn_up_W_packed, L0_ffn_up_scales,
                L0_ffn_down_W_packed, L0_ffn_down_scales):
    P = _perm16384()
    wqb = np.ascontiguousarray(
        np.asarray(L0_attn_Wqkv, dtype=np.float32)
        .reshape(16, 128, NBANDS, 128).transpose(2, 0, 1, 3))
    wob = np.ascontiguousarray(np.asarray(L0_attn_Wo, dtype=np.float32).reshape(16, 128, HID))
    gpk_ = np.ascontiguousarray(np.asarray(L0_ffn_gate_W_packed, dtype=np.int32).T[:, P])
    upk_ = np.ascontiguousarray(np.asarray(L0_ffn_up_W_packed, dtype=np.int32).T[:, P])
    dpk_ = np.ascontiguousarray(np.asarray(L0_ffn_down_W_packed, dtype=np.int32).T)
    gsc_ = np.ascontiguousarray(np.asarray(L0_ffn_gate_scales, dtype=np.float32).T[:, P])
    usc_ = np.ascontiguousarray(np.asarray(L0_ffn_up_scales, dtype=np.float32).T[:, P])
    dsc_ = np.ascontiguousarray(np.asarray(L0_ffn_down_scales, dtype=np.float32).T)
    xf = np.asarray(L0_attn_hidden_states, dtype=np.float32)
    in_maps = []
    for c in range(8):
        xs_ = np.ascontiguousarray(xf[SEQ * c:SEQ * c + SEQ])
        in_maps.append({
            "x": xs_,
            "xt": np.ascontiguousarray(xs_.transpose(0, 2, 1)),
            "wqb": wqb, "wob": wob,
            "gpk": gpk_, "upk": upk_, "dpk": dpk_,
            "gsc": gsc_, "usc": usc_, "dsc": dsc_,
        })
    return in_maps


def kernel(**inputs):
    in_maps = prep_inputs(**inputs)
    nc = build_program(dev=DEV)
    res = run_bass_kernel_spmd(nc, in_maps, core_ids=list(range(8)))
    outs = [r["out"] for r in res.results]
    return np.concatenate(outs, axis=0)
